# version 15
# baseline (speedup 1.0000x reference)
import math
from contextlib import ExitStack

import numpy as np

import concourse.bass as bass
import concourse.mybir as mybir
from concourse.bass_utils import run_bass_kernel_spmd

F32 = mybir.dt.float32
BF16 = mybir.dt.bfloat16
AF = mybir.ActivationFunctionType
ALU = mybir.AluOpType
AX = mybir.AxisListType

D = 2048
TB = 2048
TM = 1024
NH = 8
LAM_INIT = 0.8 - 0.6 * math.exp(-0.3 * 12)
INV_SQRT8 = 1.0 / math.sqrt(8.0)
EPS = 1e-5

ENGS = ("sync", "gpsimd", "vector", "scalar", "tensor")
NDMASEM = 6


class _G:

    def __init__(self):
        self.seq = {e: [] for e in ENGS}
        self.ndma = {e: 0 for e in ENGS}
        self.ncc = 0

    def op(self, eng, emit, deps=()):
        i = len(self.seq[eng])
        self.seq[eng].append({"emit": emit, "deps": list(deps), "kind": "c",
                              "inc": False})
        return ("e", eng, i)

    def dma(self, eng, emit, deps=()):
        di = self.ndma[eng]
        deps = list(deps)
        if di >= NDMASEM:
            deps.append(("d", eng, di - NDMASEM))
        self.seq[eng].append({"emit": emit, "deps": deps, "kind": "d",
                              "di": di})
        self.ndma[eng] += 1
        return ("d", eng, di)

    def coll(self, emit, deps=()):
        ci = self.ncc
        self.ncc += 1
        self.seq["gpsimd"].append({"emit": emit, "deps": list(deps),
                                   "kind": "cc", "ci": ci})
        return ("cc", ci)

    def finalize_and_emit(self, nc, stack):
        for e in ENGS:
            for o in self.seq[e]:
                for dep in o["deps"]:
                    if dep[0] == "e":
                        self.seq[dep[1]][dep[2]]["inc"] = True
        inc_at = {}
        for e in ENGS:
            n = 0
            for i, o in enumerate(self.seq[e]):
                if o["kind"] == "c" and o["inc"]:
                    n += 1
                inc_at[(e, i)] = n
        esem = {e: stack.enter_context(nc.semaphore(f"se_{e}")) for e in ENGS}
        dsem = {e: [stack.enter_context(nc.semaphore(f"sd_{e}_{k}"))
                    for k in range(NDMASEM)]
                for e in ENGS if self.ndma[e] > 0}
        csem = [stack.enter_context(nc.semaphore(f"scc_{j}"))
                for j in range(self.ncc)]

        def resolve(dep):
            if dep[0] == "e":
                _, e, i = dep
                assert self.seq[e][i]["inc"]
                return (("e", e), esem[e], inc_at[(e, i)])
            if dep[0] == "d":
                _, e, di = dep
                return (("d", e, di % NDMASEM), dsem[e][di % NDMASEM],
                        16 * (di // NDMASEM + 1))
            _, ci = dep
            return (("cc", ci), csem[ci], 1)

        block = stack.enter_context(nc.Block())

        def run_engine(engobj, ename):
            waited = {}
            for o in self.seq[ename]:
                for dep in o["deps"]:
                    key, sem, val = resolve(dep)
                    if waited.get(key, 0) < val:
                        engobj.wait_ge(sem, val)
                        waited[key] = val
                inst = o["emit"](engobj)
                if o["kind"] == "d":
                    inst.then_inc(dsem[ename][o["di"] % NDMASEM], 16)
                elif o["kind"] == "cc":
                    inst.then_inc(csem[o["ci"]], 1)
                elif o["inc"]:
                    inst.then_inc(esem[ename], 1)

        block.sync(lambda e: run_engine(e, "sync"))
        block.gpsimd(lambda e: run_engine(e, "gpsimd"))
        block.vector(lambda e: run_engine(e, "vector"))
        block.scalar(lambda e: run_engine(e, "scalar"))
        block.tensor(lambda e: run_engine(e, "tensor"))


def build_nc():
    nc = bass.Bass("TRN2", target_bir_lowering=False, debug=False)
    g = _G()

    x_full = nc.dram_tensor("x_full", [TB, D], F32, kind="ExternalInput").ap()
    x_mine = nc.dram_tensor("x_mine", [TM, D], F32, kind="ExternalInput").ap()
    wqt = nc.dram_tensor("wqt", [D, 1024], F32, kind="ExternalInput").ap()
    wkt = nc.dram_tensor("wkt", [D, 1024], F32, kind="ExternalInput").ap()
    wvt = nc.dram_tensor("wvt", [D, 1024], F32, kind="ExternalInput").ap()
    wot = nc.dram_tensor("wot", [1024, D], F32, kind="ExternalInput").ap()
    lq1 = nc.dram_tensor("lq1", [1, 64], F32, kind="ExternalInput").ap()
    lk1 = nc.dram_tensor("lk1", [1, 64], F32, kind="ExternalInput").ap()
    lq2 = nc.dram_tensor("lq2", [1, 64], F32, kind="ExternalInput").ap()
    lk2 = nc.dram_tensor("lk2", [1, 64], F32, kind="ExternalInput").ap()
    subln = nc.dram_tensor("subln", [1, 128], F32, kind="ExternalInput").ap()
    fbias = nc.dram_tensor("fbias", [1, D], F32, kind="ExternalInput").ap()
    fct = nc.dram_tensor("fct", [D, D], F32, kind="ExternalInput").ap()
    out_e = nc.dram_tensor("out", [TM, D], F32, kind="ExternalOutput").ap()

    xnm_dram = nc.dram_tensor("xnm_dram", [TM, D], BF16).ap()
    q_dram = nc.dram_tensor("q_dram", [NH, 128, TB], BF16).ap()
    k_dram = nc.dram_tensor("k_dram", [NH, 128, TB], BF16).ap()
    attn_dram = nc.dram_tensor("attn_dram", [NH, TB, 128], BF16).ap()
    fct_b = nc.dram_tensor("fct_b", [D, D], F32).ap()
    f_sum = nc.dram_tensor("f_sum", [D, D], F32, addr_space="Shared").ap()
    f_bf = nc.dram_tensor("f_bf", [D, D], BF16).ap()
    rs_in = [nc.dram_tensor(f"rs_in{h}", [TB, 1024], BF16).ap() for h in (0, 1)]
    outA = nc.dram_tensor("outA", [TM, D], F32).ap()
    rs_out = [nc.dram_tensor(f"rs_out{h}", [TM, 1024], BF16).ap() for h in (0, 1)]

    SB = 16512
    TOP = 229344

    def sb(name, shape, dtype, off):
        nbytes = shape[1] * mybir.dt.size(dtype)
        assert off + nbytes <= TOP, (name, off, nbytes)
        return nc.alloc_sbuf_tensor_at(name, shape, dtype, offset=off).ap()

    class Bump:
        def __init__(self, base):
            self.off = base

        def __call__(self, name, shape, dtype):
            nbytes = shape[1] * mybir.dt.size(dtype)
            off = (self.off + 31) // 32 * 32
            self.off = off + nbytes
            return sb(name, shape, dtype, off)

    pz = Bump(SB)
    ones_f = pz("ones_f", [1, 128], F32)
    sub_row = pz("sub_row", [1, 128], F32)
    lamv = pz("lamv", [1, 16], F32)
    lpr1 = pz("lpr1", [1, 64], F32)
    lpr2 = pz("lpr2", [1, 64], F32)
    lnb = pz("lnb", [128, 1], F32)
    swb = pz("swb", [128, 128], F32)
    _fb_off = (pz.off + 31) // 32 * 32
    fb_row = pz("fb_row", [1, D], F32)
    fb_b = pz("fb_b", [128, D], F32)
    snm = pz("snm", [128, 24], F32)
    sss = pz("sss", [128, 24], F32)
    sm2 = pz("sm2", [128, 24], F32)
    svar = pz("svar", [128, 24], F32)
    srst = pz("srst", [128, 24], F32)
    er1 = pz("er1", [128, 4], F32)
    er2 = pz("er2", [128, 4], F32)
    elr = pz("elr", [128, 4], F32)
    ess = pz("ess", [128, 4], F32)
    elnv = pz("elnv", [128, 4], F32)
    ersr = pz("ersr", [128, 4], F32)
    eu1 = [pz(f"eu1_{p}", [128, 128], F32) for p in (0, 1)]
    eac = [pz(f"eac_{p}", [128, 128], F32) for p in (0, 1)]
    esq = sb("esq", [128, 128], F32, _fb_off)
    epsb = pz("epsb", [128, 1], F32)
    idt = pz("idt", [128, 128], F32)

    A_ = (pz.off + 31) // 32 * 32
    B_ = A_ + 65536
    C_ = B_ + 32768
    D_ = C_ + 33280
    S_ = D_ + 32768
    assert S_ + 26624 <= TOP, (S_, TOP)

    xnT_all = sb("xnT_all", [128, 16 * TB], BF16, A_)
    xnT = [xnT_all[:, dc * TB:(dc + 1) * TB] for dc in range(16)]
    ptil = [sb(f"ptil_{p}", [128, 16 * 1024], BF16, A_ + p * 32768) for p in (0, 1)]
    F_sb = [sb(f"F_{dc}", [128, D], BF16, A_ + dc * 4096) for dc in range(16)]
    wtb = [sb(f"wtb_{s}", [128, 16 * 512], BF16, B_ + s * 16384) for s in (0, 1)]
    woT = [sb(f"woT_{h}", [128, D], BF16, B_ + h * 4096) for h in range(NH)]

    v_sb = [sb(f"v_{sc}", [128, 1032], BF16, C_ + sc * 2080) for sc in range(16)]
    acT = [sb(f"acT_{h}", [128, TB], BF16, C_ + h * 4096) for h in range(NH)]
    ob_bf = [sb(f"ob_{p}", [128, D], F32, C_ + p * 8192) for p in (0, 1)]
    rso_t = [[sb(f"rso{ab}_{p}", [128, 1024], BF16,
                 C_ + 16384 + (ab * 2 + p) * 2048) for p in (0, 1)]
             for ab in (0, 1)]
    xb_t = [sb(f"xb_{p}", [128, D], F32, D_ + p * 8192) for p in (0, 1)]
    xs_s2 = [sb("xs_s", [128, D], F32, D_ + 16384),
             sb("xs_sb", [128, D], F32, C_)]
    sq_scr = sb("sq_scr", [128, D], BF16, D_ + 24576)
    fstg = [sb(f"fstg_{p}", [128, D], F32, D_ + 16384 + p * 8192) for p in (0, 1)]
    rs_stage = [sb(f"rsst_{p}", [128, 1024], BF16, D_ + p * 2048) for p in (0, 1)]
    eu1 += [sb(f"eu1_{p}", [128, 128], F32, D_ + (p - 2) * 1024) for p in (2, 3)]
    eac = eac + [sb(f"eac_{p}", [128, 128], F32, D_ + (p - 2) * 1024 + 512)
                 for p in (2, 3)]
    osb = [sb(f"osb_{p}", [128, D], F32, D_ + 8192 + p * 8192) for p in (0, 1)]
    qh_t = [sb(f"qh_{p}", [128, TB], BF16, S_ + p * 4096) for p in (0, 1)]
    xmT = ([sb(f"xmT_{k}", [128, TM], BF16, S_ + k * 2048) for k in range(13)]
           + [sb(f"xmT_{k}", [128, TM], BF16, D_ + 4096 + (k - 13) * 2048)
              for k in (13, 14)]
           + [sb("xmT_15", [128, TM], BF16, _fb_off + 512)])
    xs2 = [sb("xs2_0", [128, D], BF16, S_ + 0),
           sb("xs2_1", [128, D], BF16, S_ + 16384)]
    kh_t = [sb(f"kh_{p}", [128, TB], BF16, S_ + 8192 + p * 4096) for p in (0, 1)]
    ab_t = sb("ab_t", [128, 16 * 128], BF16, S_ + 16384)
    st_t = [sb(f"st_{p}", [128, 512], BF16, S_ + 20480 + p * 1024) for p in (0, 1)]
    wst = [sb(f"wst_{p}", [128, 512], F32, S_ + 22528 + p * 2048) for p in (0, 1)]
    sub_sc = sub_row

    pool_h = nc.alloc_psum_tensor("pool", [128, 4096], F32)
    pool = pool_h.ap()

    def bank(i, lo=0, hi=512):
        return pool[:, i * 512 + lo: i * 512 + hi]

    ident_d = nc.inline_tensor(np.eye(128, dtype=np.float32), "ident128").ap()

    xld = [None] * 16

    def emit_xld(i2, deps=()):
        xld[i2] = [g.dma("scalar", (lambda i2, q: lambda e: e.dma_start(
            out=xb_t[i2 % 2][:, q * 512:(q + 1) * 512],
            in_=x_full[i2 * 128:(i2 + 1) * 128, q * 512:(q + 1) * 512]))(i2, q),
            deps=deps) for q in range(4)]

    for i0 in (0, 1):
        emit_xld(i0)
    t_idt = g.dma("sync", lambda e: e.dma_start(out=idt[:, :], in_=ident_d[:, :]))
    t_subrow = g.dma("sync", lambda e: e.dma_start(out=sub_row[:, :], in_=subln[:, :]))
    t_lq1 = g.dma("sync", lambda e: e.dma_start(out=lpr1[:, :], in_=lq1[:, :]))
    t_lk1 = g.dma("sync", lambda e: e.dma_start(out=lpr2[:, :], in_=lk1[:, :]))
    t_fbrow = g.dma("sync", lambda e: e.dma_start(out=fb_row[:, :], in_=fbias[:, :]))

    t_ones = g.op("vector", lambda e: e.memset(ones_f[:, :], 1.0))
    t_eps = g.op("vector", lambda e: e.memset(epsb[:, :], EPS))
    t_m1 = g.op("vector", lambda e: e.tensor_mul(lpr1[:, :], lpr1[:, :], lpr2[:, :]),
                deps=[t_lq1, t_lk1])
    t_s1 = g.op("vector", lambda e: e.reduce_sum(lamv[:, 0:1], lpr1[:, :], axis=AX.X),
                deps=[t_m1])
    t_lq2 = g.dma("sync", lambda e: e.dma_start(out=lpr1[:, :], in_=lq2[:, :]),
                  deps=[t_s1])
    t_lk2 = g.dma("sync", lambda e: e.dma_start(out=lpr2[:, :], in_=lk2[:, :]),
                  deps=[t_s1])
    t_m2 = g.op("vector", lambda e: e.tensor_mul(lpr1[:, :], lpr1[:, :], lpr2[:, :]),
                deps=[t_lq2, t_lk2])
    t_s2 = g.op("vector", lambda e: e.reduce_sum(lamv[:, 1:2], lpr1[:, :], axis=AX.X),
                deps=[t_m2])
    t_e1 = g.op("scalar", lambda e: e.activation(lamv[:, 2:3], lamv[:, 0:1], AF.Exp),
                deps=[t_s1])
    t_e2 = g.op("scalar", lambda e: e.activation(lamv[:, 3:4], lamv[:, 1:2], AF.Exp),
                deps=[t_s2])
    t_ln0 = g.op("vector", lambda e: e.tensor_sub(lamv[:, 4:5], lamv[:, 3:4], lamv[:, 2:3]),
                 deps=[t_e1, t_e2])
    t_ln1 = g.op("vector", lambda e: e.tensor_scalar(
        out=lamv[:, 5:6], in0=lamv[:, 4:5], scalar1=-LAM_INIT, scalar2=None,
        op0=ALU.add), deps=[t_ln0])
    t_subsc = g.op("vector", lambda e: e.tensor_scalar(
        out=sub_sc[:, :], in0=sub_row[:, :], scalar1=(1.0 - LAM_INIT), scalar2=None,
        op0=ALU.mult), deps=[t_subrow])
    t_bsw = g.op("tensor", lambda e: e.matmul(
        bank(7, 0, 128), ones_f[0:1, :], sub_sc[0:1, :], start=True, stop=True),
        deps=[t_ones, t_subsc])
    t_bln = g.op("tensor", lambda e: e.matmul(
        bank(7, 128, 129), ones_f[0:1, :], lamv[0:1, 5:6], start=True, stop=True),
        deps=[t_ones, t_ln1])
    t_esw = g.op("vector", lambda e: e.tensor_copy(swb[:, :], bank(7, 0, 128)),
                 deps=[t_bsw, t_bln])
    t_eln = g.op("vector", lambda e: e.tensor_copy(lnb[:, :], bank(7, 128, 129)),
                 deps=[t_bln])
    t_fbb = []
    prev = None
    for eo in range(4):
        dep = [t_ones, t_fbrow] + ([prev] if prev else [])
        tb_ = g.op("tensor", (lambda eo: lambda e: e.matmul(
            bank(6), ones_f[0:1, :], fb_row[0:1, eo * 512:(eo + 1) * 512],
            start=True, stop=True))(eo), deps=dep)
        te_ = g.op("vector", (lambda eo: lambda e: e.tensor_copy(
            fb_b[:, eo * 512:(eo + 1) * 512], bank(6)))(eo), deps=[tb_])
        t_fbb.append(te_)
        prev = te_

    norm_tok = [None] * 24
    sq_tok = [None] * 24
    store_tok = [None] * 24
    prev_sq = [None]
    prev_psq = [None]
    xs_war = [None, None]
    xm_war = [None, None]
    xnT_ev = {}
    banklast = [None] * 8
    tb_rot = [0]

    def ln_tile(i):
        mine = i >= 16
        j = i - 16 if mine else i
        src_ = x_mine if mine else x_full
        xb = xb_t[i % 2]
        c = i
        if mine:
            war = [norm_tok[i - 2], sq_tok[i - 2]]
            t_ld = [g.dma("sync", (lambda xb, j, q: lambda e: e.dma_start(
                out=xb[:, q * 512:(q + 1) * 512],
                in_=src_[j * 128:(j + 1) * 128, q * 512:(q + 1) * 512]))(xb, j, q),
                deps=war) for q in range(4)]
        else:
            t_ld = xld[i]
        ceng = "vector"
        t_sum = g.op(ceng, (lambda xb, c: lambda e: e.reduce_sum(
            snm[:, c:c + 1], xb[:, :], axis=AX.X))(xb, c), deps=t_ld)
        t_nm = g.op(ceng, (lambda c: lambda e: e.tensor_scalar(
            out=snm[:, c:c + 1], in0=snm[:, c:c + 1], scalar1=-1.0 / D,
            scalar2=None, op0=ALU.mult))(c), deps=[t_sum])
        sqdeps = list(t_ld) + ([prev_sq[0]] if prev_sq[0] else [])
        t_sq = g.op("scalar", (lambda xb, c: lambda e: e.activation(
            sq_scr[:, :], xb[:, :], AF.Square, accum_out=sss[:, c:c + 1]))(xb, c),
            deps=sqdeps)
        prev_sq[0] = t_sq
        sq_tok[i] = t_sq
        t_mm2 = g.op(ceng, (lambda c: lambda e: e.tensor_scalar(
            out=sm2[:, c:c + 1], in0=snm[:, c:c + 1], scalar1=snm[:, c:c + 1],
            scalar2=None, op0=ALU.mult))(c), deps=[t_nm])
        t_var = g.op(ceng, (lambda c: lambda e: e.scalar_tensor_tensor(
            out=svar[:, c:c + 1], in0=sss[:, c:c + 1], scalar=1.0 / D,
            in1=sm2[:, c:c + 1], op0=ALU.mult, op1=ALU.subtract))(c),
            deps=[t_sq, t_mm2])
        t_lnv = g.op("scalar", (lambda c: lambda e: e.activation(
            svar[:, c:c + 1], svar[:, c:c + 1], AF.Ln, bias=epsb[:, :]))(c),
            deps=[t_var, t_eps])
        t_rst = g.op("scalar", (lambda c: lambda e: e.activation(
            srst[:, c:c + 1], svar[:, c:c + 1], AF.Exp, scale=-0.5))(c), deps=[t_lnv])
        if mine:
            t_rst = g.op("vector", (lambda c: lambda e: e.tensor_scalar(
                out=srst[:, c:c + 1], in0=srst[:, c:c + 1], scalar1=INV_SQRT8,
                scalar2=None, op0=ALU.mult))(c), deps=[t_rst])
        if mine:
            xso = xs2[j % 2]
            ndeps = [t_nm, t_rst]
            if xm_war[j % 2] is not None:
                ndeps.append(xm_war[j % 2])
            t_no = g.op(ceng, (lambda xb, c, xso: lambda e: e.tensor_scalar(
                out=xso[:, :], in0=xb[:, :], scalar1=snm[:, c:c + 1],
                scalar2=srst[:, c:c + 1], op0=ALU.add, op1=ALU.mult))(xb, c, xso),
                deps=ndeps)
            norm_tok[i] = t_no
            t_st = g.dma("sync", (lambda j, xso: lambda e: e.dma_start(
                out=xnm_dram[j * 128:(j + 1) * 128, :], in_=xso[:, :]))(j, xso),
                deps=[t_no])
            store_tok[i] = t_st
            xm_war[j % 2] = t_st
            return
        xp = i % 2
        xs_s = xs_s2[xp]
        ndeps = [t_nm, t_rst]
        if xs_war[xp] is not None:
            ndeps.append(xs_war[xp])
        t_no = g.op(ceng, (lambda xb, c, xs_s: lambda e: e.tensor_scalar(
            out=xs_s[:, :], in0=xb[:, :], scalar1=snm[:, c:c + 1],
            scalar2=srst[:, c:c + 1], op0=ALU.add, op1=ALU.mult))(xb, c, xs_s),
            deps=ndeps)
        norm_tok[i] = t_no
        if i + 2 < 16:
            emit_xld(i + 2, deps=[t_no, t_sq])
        if True:
            last_tr = None
            for dcg in range(4):
                base = ((i * 4 + dcg) % 2) * 4
                trs = []
                for k in range(4):
                    dcb = dcg * 4 + k
                    tb = base + k
                    tdeps = [t_no, t_idt]
                    if banklast[tb] is not None:
                        tdeps.append(banklast[tb])
                    t_tr = g.op("tensor", (lambda dcb, tb, xs_s: lambda e: e.transpose(
                        bank(tb, 0, 128), xs_s[:, dcb * 128:(dcb + 1) * 128],
                        idt[:, :]))(dcb, tb, xs_s), deps=tdeps)
                    trs.append(t_tr)
                    last_tr = t_tr
                t_evt = g.op("scalar", (lambda dcg, i, base: lambda e: e.copy(
                    xnT_all.rearrange("p (dcb c) -> p dcb c", c=TB)
                        [:, dcg * 4:(dcg + 1) * 4, i * 128:(i + 1) * 128],
                    pool.rearrange("p (b c) -> p b c", c=512)
                        [:, base:base + 4, 0:128]))(dcg, i, base),
                    deps=trs)
                for k in range(4):
                    banklast[base + k] = t_evt
                    xnT_ev[(i, dcg * 4 + k)] = t_evt
            xs_war[xp] = last_tr

    wsrc = (wqt, wkt, wvt)
    wt_cast = {}
    wt_last_reader = [None, None]
    wst_last = [None, None]

    def w_slice(si):
        p, eh = si // 2, si % 2
        slot = si % 2
        casts = []
        for dc in range(16):
            ldeps = []
            if wst_last[dc % 2] is not None:
                ldeps.append(wst_last[dc % 2])
            t_l = g.dma("sync", (lambda p, eh, dc: lambda e: e.dma_start(
                out=wst[dc % 2][:, :],
                in_=wsrc[p][dc * 128:(dc + 1) * 128, eh * 512:(eh + 1) * 512]))(p, eh, dc),
                deps=ldeps)
            t_c = g.op("gpsimd", (lambda slot, dc: lambda e: e.tensor_copy(
                wtb[slot][:, dc * 512:(dc + 1) * 512], wst[dc % 2][:, :]))(slot, dc),
                deps=[t_l])
            wst_last[dc % 2] = t_c
            casts.append(t_c)
        wt_cast[si] = casts

    for i in range(4):
        ln_tile(i)
    w_slice(0)
    for i in range(4, 8):
        ln_tile(i)
    w_slice(1)
    for i in range(8, 16):
        ln_tile(i)
    for i in range(16, 24):
        ln_tile(i)

    t_vmem = []
    for sc in range(16):
        deps = [xs_war[1]] if sc < 5 else []
        t_vmem.append(g.op("vector", (lambda sc: lambda e: e.memset(
            v_sb[sc][:, :], 1.0))(sc), deps=deps))

    w_slice(2)
    w_slice(3)
    w_slice(4)
    w_slice(5)

    t_fstage = g.dma("gpsimd", lambda e: e.dma_start(out=fct_b[:, :], in_=fct[:, :]))
    t_ar = g.coll(lambda e: e.collective_compute(
        "AllReduce", ALU.add, replica_groups=[list(range(8))],
        ins=[fct_b[:, :]], outs=[f_sum[:, :]]), deps=[t_fstage])

    q_store = {}
    k_store = {}
    v_evac = []
    slice_last_mm = {}
    if banklast[6] is None:
        banklast[6] = t_fbb[-1]
    if banklast[7] is None:
        banklast[7] = t_eln
    stage_last = [None, None]

    for si in range(6):
        p, eh = si // 2, si % 2
        slot = si % 2
        tws = wt_cast[si]
        if p < 2:
            for ts in range(4):
                for ec in range(4):
                    b = (ts % 2) * 4 + ec
                    head = eh * 4 + ec
                    grp_deps = tws + [xnT_ev[(ts * 4 + k, dc)]
                                      for k in range(4) for dc in range(16)]
                    if banklast[b] is not None:
                        grp_deps = grp_deps + [banklast[b]]
                    last_mm = None
                    for dc in range(16):
                        tok = g.op("tensor", (lambda slot, dc, ec, ts, b: lambda e:
                            e.matmul(bank(b),
                                     wtb[slot][:, dc * 512 + ec * 128: dc * 512 + (ec + 1) * 128],
                                     xnT[dc][:, ts * 512:(ts + 1) * 512],
                                     start=(dc == 0), stop=(dc == 15)))(slot, dc, ec, ts, b),
                            deps=(grp_deps if dc == 0 else []))
                        last_mm = tok
                    wt_last_reader[slot] = last_mm
                    slice_last_mm[si] = last_mm
                    sb_i = (ts * 4 + ec) % 2
                    ev_deps = [last_mm]
                    if stage_last[sb_i] is not None:
                        ev_deps.append(stage_last[sb_i])
                    t_ev = g.op("vector", (lambda sb_i, b: lambda e: e.tensor_copy(
                        st_t[sb_i][:, :], bank(b)))(sb_i, b), deps=ev_deps)
                    banklast[b] = t_ev
                    dstd = q_dram if p == 0 else k_dram
                    t_sto = g.dma("scalar", (lambda sb_i, dstd, head, ts: lambda e:
                        e.dma_start(out=dstd[head, :, ts * 512:(ts + 1) * 512],
                                    in_=st_t[sb_i][:, :]))(sb_i, dstd, head, ts),
                        deps=[t_ev])
                    stage_last[sb_i] = t_sto
                    store = q_store if p == 0 else k_store
                    store.setdefault(head, []).append(t_sto)
        else:
            for sc in range(16):
                b = sc % 8
                grp_deps = tws + [xnT_ev[((sc // 4) * 4 + k, dc)]
                                  for k in range(4) for dc in range(16)]
                if banklast[b] is not None:
                    grp_deps = grp_deps + [banklast[b]]
                last_mm = None
                for dc in range(16):
                    tok = g.op("tensor", (lambda slot, dc, sc, b: lambda e:
                        e.matmul(bank(b),
                                 xnT[dc][:, sc * 128:(sc + 1) * 128],
                                 wtb[slot][:, dc * 512:(dc + 1) * 512],
                                 start=(dc == 0), stop=(dc == 15)))(slot, dc, sc, b),
                        deps=(grp_deps if dc == 0 else []))
                    last_mm = tok
                wt_last_reader[slot] = last_mm
                slice_last_mm[si] = last_mm
                t_ev = g.op("vector", (lambda sc, b, eh: lambda e: e.tensor_copy(
                    v_sb[sc][:, eh * 516: eh * 516 + 516]
                        .rearrange("p (h q) -> p h q", q=129)[:, :, 0:128],
                    bank(b).rearrange("p (h q) -> p h q", q=128)))(sc, b, eh),
                    deps=[last_mm, t_vmem[sc]])
                banklast[b] = t_ev
                v_evac.append(t_ev)

    for si in range(2, 6):
        first = wt_cast[si][0]
        g.seq[first[1]][first[2]]["deps"].append(slice_last_mm[si - 2])

    qk_tok = {}
    qh_last = [None, None]
    kh_last = [None, None]

    def load_head(h, extra=()):
        hp = h % 2
        qdeps = q_store[h] + list(extra) + ([qh_last[hp]] if qh_last[hp] else [])
        t_qh = g.dma("sync", (lambda h, hp: lambda e: e.dma_start(
            out=qh_t[hp][:, :], in_=q_dram[h, :, :]))(h, hp), deps=qdeps)
        kdeps = k_store[h] + ([kh_last[hp]] if kh_last[hp] else [])
        t_kh = g.dma("sync", (lambda h, hp: lambda e: e.dma_start(
            out=kh_t[hp][:, :], in_=k_dram[h, :, :]))(h, hp), deps=kdeps)
        qk_tok[h] = (t_qh, t_kh)

    load_head(0, extra=[store_tok[22]])
    load_head(1)

    a_free = [wt_last_reader[0], wt_last_reader[1]]
    ab_free = [store_tok[23]]
    ab_last = [None]
    epi_last = [None, None, None, None]
    ptil_last = [None, None]
    attn_store = [None] * NH
    sq_free = [t_fbb[-1]]
    units = [(h, tt) for h in range(NH) for tt in range(4)]
    score_info = {}

    def emit_score_group(ui, j):
        h, tt = units[ui]
        S = 4 * (ui % 2)
        if j == 0 and tt == 0 and h + 1 < NH and (h + 1) not in qk_tok:
            load_head(h + 1)
        hp = h % 2
        t_qh, t_kh = qk_tok[h]
        out = []
        for sc in (2 * j, 2 * j + 1):
            b1 = S + 2 * (sc % 2)
            b2 = b1 + 1
            mm_deps1 = [t_qh, t_kh]
            if banklast[b1] is not None:
                mm_deps1 = mm_deps1 + [banklast[b1]]
            t_s1m = g.op("tensor", (lambda hp, sc, tt, b1, ui=ui: lambda e: e.matmul(
                bank(b1, 0, 512),
                kh_t[hp][0:64, sc * 128:(sc + 1) * 128],
                qh_t[hp][0:64, tt * 512:(tt + 1) * 512],
                start=True, stop=True, tile_position=(0, 0)).annotate(
                f"s1m_u{ui}_sc{sc}"))(hp, sc, tt, b1),
                deps=mm_deps1)
            mm_deps2 = []
            if banklast[b2] is not None:
                mm_deps2 = [banklast[b2]]
            t_s2m = g.op("tensor", (lambda hp, sc, tt, b2: lambda e: e.matmul(
                bank(b2, 0, 512),
                kh_t[hp][64:128, sc * 128:(sc + 1) * 128],
                qh_t[hp][64:128, tt * 512:(tt + 1) * 512],
                start=True, stop=True, tile_position=(64, 0)))(hp, sc, tt, b2),
                deps=mm_deps2)
            out.append((t_s1m, t_s2m, b1))
            if tt == 3 and sc == 15:
                qh_last[hp] = t_s2m
                kh_last[hp] = t_s2m
        return out

    def emit_exps(ui, mms):
        par = ui % 2
        exp_toks = []
        for sc in range(16):
            t_s1m, t_s2m, b1 = mms[sc]
            edeps = [t_s1m, t_s2m]
            if ptil_last[par] is not None:
                edeps.append(ptil_last[par])
            if a_free:
                edeps = edeps + a_free
            t_ex = g.op("scalar", (lambda par, sc, b1, ui=ui: lambda e: e.activation(
                out=ptil[par][:, sc * 1024:(sc + 1) * 1024]
                    .rearrange("p (b c) -> p b c", c=512),
                in_=pool[:, b1 * 512:(b1 + 2) * 512]
                    .rearrange("p (b c) -> p b c", c=512),
                func=AF.Exp, scale=0.125).annotate(f"exp_u{ui}_sc{sc}"))(par, sc, b1),
                deps=edeps)
            exp_toks.append(t_ex)
            banklast[b1] = t_ex
            banklast[b1 + 1] = t_ex
            if sc + 2 < 16:
                nxt = mms[sc + 2][0]
                g.seq[nxt[1]][nxt[2]]["deps"].append(t_ex)
        if a_free:
            a_free.clear()
        score_info[ui] = exp_toks

    def emit_av_chunk(ui, tsub, half, av_first):
        h, tt = units[ui]
        par = ui % 2
        S = 4 * (ui % 2)
        a1 = S + (tsub % 2) * 2
        a2 = a1 + 1
        last = None
        for sc in range(half * 8, half * 8 + 8):
            deps1 = []
            deps2 = []
            if sc == 0:
                deps1 = list(score_info[ui])
                for btok in (banklast[a1], banklast[a2]):
                    if btok is not None:
                        deps1.append(btok)
                deps1 += v_evac
                if banklast[a2] is not None:
                    deps2 = [banklast[a2]]
            tok1 = g.op("tensor", (lambda par, sc, tsub, h, a1: lambda e:
                e.matmul(bank(a1, 0, 129),
                         ptil[par][:, sc * 1024 + tsub * 128: sc * 1024 + (tsub + 1) * 128],
                         v_sb[sc][:, h * 129: (h + 1) * 129],
                         start=(sc == 0), stop=(sc == 15)))(par, sc, tsub, h, a1),
                deps=deps1)
            tok2 = g.op("tensor", (lambda par, sc, tsub, h, a2: lambda e:
                e.matmul(bank(a2, 0, 129),
                         ptil[par][:, sc * 1024 + 512 + tsub * 128: sc * 1024 + 512 + (tsub + 1) * 128],
                         v_sb[sc][:, h * 129: (h + 1) * 129],
                         start=(sc == 0), stop=(sc == 15)))(par, sc, tsub, h, a2),
                deps=deps2)
            last = (tok1, tok2)
        return last

    def emit_dve_chain(ui, tsub, last_mms):
        par = ui % 2
        S = 4 * (ui % 2)
        a1 = S + (tsub % 2) * 2
        a2 = a1 + 1
        last_a1, last_a2 = last_mms
        ptil_last[par] = last_a2
        edeps = [last_a1, last_a2, t_eln]
        if epi_last[tsub] is not None:
            edeps.append(epi_last[tsub])
        t_r1 = g.op("vector", (lambda tsub, a1: lambda e: e.reciprocal(
            er1[:, tsub:tsub + 1], bank(a1, 128, 129)))(tsub, a1), deps=edeps)
        t_r2 = g.op("vector", (lambda tsub, a2: lambda e: e.reciprocal(
            er2[:, tsub:tsub + 1], bank(a2, 128, 129)))(tsub, a2), deps=[t_r1])
        t_lr = g.op("vector", (lambda tsub: lambda e: e.tensor_mul(
            elr[:, tsub:tsub + 1], er2[:, tsub:tsub + 1], lnb[:, :]))(tsub),
            deps=[t_r2])
        t_u1 = g.op("vector", (lambda tsub, a1: lambda e: e.tensor_scalar(
            out=eu1[tsub][:, :], in0=bank(a1, 0, 128),
            scalar1=er1[:, tsub:tsub + 1], scalar2=None, op0=ALU.mult))(tsub, a1),
            deps=[t_lr])
        t_ac = g.op("vector", (lambda tsub, a2: lambda e: e.scalar_tensor_tensor(
            out=eac[tsub][:, :], in0=bank(a2, 0, 128),
            scalar=elr[:, tsub:tsub + 1], in1=eu1[tsub][:, :],
            op0=ALU.mult, op1=ALU.add))(tsub, a2), deps=[t_u1])
        banklast[a1] = t_ac
        banklast[a2] = t_ac
        ssdeps = [t_ac]
        if sq_free[0] is not None:
            ssdeps.append(sq_free[0])
            sq_free[0] = None
        t_ss = g.op("vector", (lambda tsub: lambda e: e.scalar_tensor_tensor(
            out=esq[:, :], in0=eac[tsub][:, :], scalar=1.0, in1=eac[tsub][:, :],
            op0=ALU.mult, op1=ALU.mult, accum_out=ess[:, tsub:tsub + 1]))(tsub),
            deps=ssdeps)
        return t_ss, t_ac

    def emit_epi_act(ui, ss_toks):
        t_lv = g.op("scalar", lambda e: e.activation(
            elnv[:, :], ess[:, :], AF.Ln, scale=1.0 / 128.0, bias=epsb[:, :]),
            deps=ss_toks + [t_eps])
        t_rr = g.op("scalar", lambda e: e.activation(
            ersr[:, :], elnv[:, :], AF.Exp, scale=-0.5), deps=[t_lv])
        return t_rr

    def emit_fins(ui, t_rr, ac_info):
        h, tt = units[ui]
        head_finals = []
        for tsub in range(4):
            abdeps = [t_rr, t_esw, ac_info[tsub]]
            if ab_last[0] is not None and tt == 0 and tsub == 0:
                abdeps.append(ab_last[0])
            if ab_free:
                abdeps += ab_free
                ab_free.clear()
            t_fin = g.op("vector", (lambda tsub, tt: lambda e:
                e.scalar_tensor_tensor(
                    out=ab_t[:, (tt * 4 + tsub) * 128:(tt * 4 + tsub + 1) * 128],
                    in0=eac[tsub][:, :], scalar=ersr[:, tsub:tsub + 1], in1=swb[:, :],
                    op0=ALU.mult, op1=ALU.mult))(tsub, tt), deps=abdeps)
            epi_last[tsub] = t_fin
            head_finals.append(t_fin)
        if tt == 3:
            t_asto = g.dma("scalar", (lambda h: lambda e: e.dma_start(
                out=attn_dram[h].rearrange("(c p) e -> p c e", p=128),
                in_=ab_t[:, :].rearrange("p (c e) -> p c e", e=128)))(h),
                deps=head_finals)
            attn_store[h] = t_asto
            ab_last[0] = t_asto

    mms0 = []
    for j in range(8):
        mms0 += emit_score_group(0, j)
    emit_exps(0, mms0)
    for ui in range(len(units)):
        nxt = ui + 1 < len(units)
        mms = []
        ss_toks = []
        ac_info = []
        for j in range(8):
            if nxt:
                mms += emit_score_group(ui + 1, j)
            tsub, half = j // 2, j % 2
            last_mms = emit_av_chunk(ui, tsub, half, av_first=(j == 0))
            if half == 1:
                t_ss, t_ac = emit_dve_chain(ui, tsub, last_mms)
                ss_toks.append(t_ss)
                ac_info.append(t_ac)
        score_info.pop(ui)
        if nxt:
            emit_exps(ui + 1, mms)
        t_rr = emit_epi_act(ui, ss_toks)
        emit_fins(ui, t_rr, ac_info)

    c_free = [t for t in ptil_last if t is not None]
    fstg_last = [None, None]
    fstg_frees = [[xs_war[0]], [sq_tok[23]]]
    t_woc = []
    for h in range(NH):
        fp = h % 2
        ldeps = list(fstg_frees[fp])
        fstg_frees[fp] = []
        if fstg_last[fp] is not None:
            ldeps.append(fstg_last[fp])
        t_l = g.dma("sync", (lambda h, fp: lambda e: e.dma_start(
            out=fstg[fp][:, :], in_=wot[h * 128:(h + 1) * 128, :]))(h, fp),
            deps=ldeps)
        t_c = g.op("gpsimd", (lambda h, fp: lambda e: e.tensor_copy(
            woT[h][:, :], fstg[fp][:, :]))(h, fp),
            deps=[t_l, wt_last_reader[h % 2]])
        fstg_last[fp] = t_c
        t_woc.append(t_c)

    t_acT = []
    for h in range(NH):
        deps = [attn_store[h]] + c_free
        t_acT.append(g.dma("sync", (lambda h: lambda e: e.dma_start(
            out=acT[h][:, :], in_=attn_dram[h], transpose=True))(h), deps=deps))

    t_fc = g.dma("gpsimd", lambda e: e.dma_start(out=f_bf[:, :], in_=f_sum[:, :]),
                 deps=[t_ar])
    af_deps = [t for t in ptil_last if t is not None]
    t_F = []
    for dc in range(16):
        t_F.append(g.dma("sync", (lambda dc: lambda e: e.dma_start(
            out=F_sb[dc][:, :], in_=f_bf[dc * 128:(dc + 1) * 128, :]))(dc),
            deps=[t_fc] + af_deps))

    rs_tok = [None, None]
    rstage_last = [None, None]
    rs_store_tok = [[], []]

    xmT_frees = {0: [qh_last[0]], 1: [qh_last[0]], 2: [qh_last[1]],
                 3: [qh_last[1]], 4: [kh_last[0]], 5: [kh_last[0]],
                 6: [kh_last[1]], 7: [kh_last[1]], 8: [ab_last[0]],
                 9: [ab_last[0]], 10: [stage_last[0], stage_last[1]],
                 11: [wst_last[0]], 12: [wst_last[1]],
                 13: [norm_tok[22], sq_tok[22]], 14: [norm_tok[22], sq_tok[22]],
                 15: [t_fbb[-1]]}
    t_xmT = []
    for dc in range(16):
        t_xmT.append(g.dma("sync", (lambda dc: lambda e: e.dma_start(
            out=xmT[dc][:, :], in_=xnm_dram[:, dc * 128:(dc + 1) * 128],
            transpose=True))(dc),
            deps=[store_tok[16 + k] for k in range(8)] + xmT_frees[dc]))

    def p4_group(eoh, tc):
        tp = tc % 2
        grp = []
        for eo2 in range(2):
            b = tp * 2 + eo2
            deps0 = t_acT + t_woc
            if banklast[b] is not None:
                deps0 = deps0 + [banklast[b]]
            last_mm = None
            for ea in range(NH):
                tok = g.op("tensor", (lambda ea, tc, eoh, eo2, b: lambda e:
                    e.matmul(bank(b),
                             acT[ea][:, tc * 128:(tc + 1) * 128],
                             woT[ea][:, (eoh * 1024 + eo2 * 512):(eoh * 1024 + (eo2 + 1) * 512)],
                             start=(ea == 0), stop=(ea == 7)))(ea, tc, eoh, eo2, b),
                    deps=(deps0 if ea == 0 else []))
                last_mm = tok
            grp.append((last_mm, b))
        evdeps = [t for t, _ in grp]
        if rstage_last[tp] is not None:
            evdeps.append(rstage_last[tp])
        else:
            evdeps += [norm_tok[22], sq_tok[22], epi_last[2], epi_last[3]]
        t_ev0 = g.op("vector", (lambda tp, b0=grp[0][1]: lambda e:
            e.tensor_copy(rs_stage[tp][:, 0:512], bank(b0)))(tp), deps=evdeps)
        t_ev1 = g.op("vector", (lambda tp, b1=grp[1][1]: lambda e:
            e.tensor_copy(rs_stage[tp][:, 512:1024], bank(b1)))(tp), deps=[t_ev0])
        banklast[grp[0][1]] = t_ev1
        banklast[grp[1][1]] = t_ev1
        t_rst_ = g.dma("scalar", (lambda tp, eoh, tc: lambda e: e.dma_start(
            out=rs_in[eoh][tc * 128:(tc + 1) * 128, :], in_=rs_stage[tp][:, :]))(tp, eoh, tc),
            deps=[t_ev1])
        rstage_last[tp] = t_rst_
        rs_store_tok[eoh].append(t_rst_)
        return grp

    for tc in range(16):
        grp_tok = p4_group(0, tc)
    rs_tok[0] = g.coll(lambda e: e.collective_compute(
        "ReduceScatter", ALU.add,
        replica_groups=[[0, 1], [2, 3], [4, 5], [6, 7]],
        ins=[rs_in[0][:, :]], outs=[rs_out[0][:, :]]), deps=rs_store_tok[0])

    xm_tok = [None, None]
    oA_store = [None]
    osb0_free = [norm_tok[23], sq_tok[23]]
    fstg_read = [None, None]

    def ffn_combineA(tcf):
        tp = tcf % 2
        xdeps = [fstg_last[tp]]
        if xm_tok[tp] is not None:
            xdeps.append(xm_tok[tp])
        t_xm = g.dma("sync", (lambda tcf, tp: lambda e: e.dma_start(
            out=fstg[tp][:, :], in_=x_mine[tcf * 128:(tcf + 1) * 128, :]))(tcf, tp),
            deps=xdeps)
        mm_last = []
        for eo in range(4):
            b = 4 + eo
            deps0 = t_F + t_xmT
            if banklast[b] is not None:
                deps0 = deps0 + [banklast[b]]
            last_mm = None
            for dc in range(16):
                tok = g.op("tensor", (lambda dc, tcf, eo, b: lambda e: e.matmul(
                    bank(b), xmT[dc][:, tcf * 128:(tcf + 1) * 128],
                    F_sb[dc][:, eo * 512:(eo + 1) * 512],
                    start=(dc == 0), stop=(dc == 15)))(dc, tcf, eo, b),
                    deps=(deps0 if dc == 0 else []))
                last_mm = tok
            mm_last.append((last_mm, b))
        o2s = []
        for eo in range(4):
            last_mm, b = mm_last[eo]
            deps = [last_mm, t_fbb[eo]]
            if oA_store[0] is not None:
                deps.append(oA_store[0])
            else:
                deps += osb0_free
            t_o1 = g.op("vector", (lambda eo, b: lambda e: e.scalar_tensor_tensor(
                out=osb[0][:, eo * 512:(eo + 1) * 512], in0=bank(b), scalar=1.0,
                in1=fb_b[:, eo * 512:(eo + 1) * 512], op0=ALU.mult, op1=ALU.add))(eo, b),
                deps=deps)
            banklast[b] = t_o1
            t_o2 = g.op("vector", (lambda tp, eo: lambda e: e.tensor_add(
                osb[0][:, eo * 512:(eo + 1) * 512],
                osb[0][:, eo * 512:(eo + 1) * 512],
                fstg[tp][:, eo * 512:(eo + 1) * 512]))(tp, eo), deps=[t_o1, t_xm])
            o2s.append(t_o2)
        fstg_read[tp] = o2s[-1]
        xm_tok[tp] = o2s[-1]
        t_oA = g.dma("scalar", (lambda tcf: lambda e: e.dma_start(
            out=outA[tcf * 128:(tcf + 1) * 128, :], in_=osb[0][:, :]))(tcf),
            deps=o2s)
        oA_store[0] = t_oA

    for tc in range(16):
        p4_group(1, tc)
        if tc % 2 == 1:
            ffn_combineA(tc // 2)
    rs_tok[1] = g.coll(lambda e: e.collective_compute(
        "ReduceScatter", ALU.add,
        replica_groups=[[0, 1], [2, 3], [4, 5], [6, 7]],
        ins=[rs_in[1][:, :]], outs=[rs_out[1][:, :]]), deps=rs_store_tok[1])
    b_free = [banklast[0], banklast[1]]

    out_store = []
    ob_last = [None, None]
    rso_last = [[None, None], [None, None]]
    osb_last = [oA_store[0], None]
    osb1_free = [fstg_read[0]]
    for tc in range(8):
        tp = tc % 2
        obdeps = [banklast[0], banklast[1], oA_store[0]]
        if ob_last[tp] is not None:
            obdeps.append(ob_last[tp])
        t_ob = g.dma("sync", (lambda tc, tp: lambda e: e.dma_start(
            out=ob_bf[tp][:, :], in_=outA[tc * 128:(tc + 1) * 128, :]))(tc, tp),
            deps=obdeps)
        t_rs = []
        for ab in range(2):
            deps = [rs_tok[ab], banklast[0], banklast[1]]
            if rso_last[ab][tp] is not None:
                deps.append(rso_last[ab][tp])
            t_rs.append(g.dma("sync", (lambda ab, tc, tp: lambda e: e.dma_start(
                out=rso_t[ab][tp][:, :],
                in_=rs_out[ab][tc * 128:(tc + 1) * 128, :]))(ab, tc, tp), deps=deps))
        st_toks = []
        for eo in range(4):
            ab = eo // 2
            deps = [t_ob, t_rs[ab]]
            if osb_last[tp] is not None:
                deps.append(osb_last[tp])
            if tp == 1 and osb1_free:
                deps += osb1_free
                osb1_free.clear()
            t_b = g.op("vector", (lambda tp, eo, ab: lambda e: e.tensor_add(
                osb[tp][:, eo * 512:(eo + 1) * 512],
                ob_bf[tp][:, eo * 512:(eo + 1) * 512],
                rso_t[ab][tp][:, (eo % 2) * 512:(eo % 2 + 1) * 512]))(tp, eo, ab),
                deps=deps)
            st_toks.append(t_b)
        ob_last[tp] = st_toks[-1]
        rso_last[0][tp] = st_toks[1]
        rso_last[1][tp] = st_toks[3]
        t_out = g.dma("scalar", (lambda tp, tc: lambda e: e.dma_start(
            out=out_e[tc * 128:(tc + 1) * 128, :], in_=osb[tp][:, :]))(tp, tc),
            deps=st_toks)
        osb_last[tp] = t_out
        out_store.append(t_out)

    g.op("scalar", lambda e: e.nop(), deps=out_store)

    stack = ExitStack()
    g.finalize_and_emit(nc, stack)
    stack.close()
    return nc


_NC_CACHE = None


def _get_nc():
    global _NC_CACHE
    if _NC_CACHE is None:
        _NC_CACHE = build_nc()
    return _NC_CACHE


def make_in_maps(inputs):
    x = np.ascontiguousarray(np.asarray(inputs["L0_attn_x"], dtype=np.float32))
    Wq = np.asarray(inputs["L0_attn_Wq"], dtype=np.float32)
    Wk = np.asarray(inputs["L0_attn_Wk"], dtype=np.float32)
    Wv = np.asarray(inputs["L0_attn_Wv"], dtype=np.float32)
    Wo = np.asarray(inputs["L0_attn_Wo"], dtype=np.float32)
    lq1 = np.asarray(inputs["L0_attn_lambda_q1"], dtype=np.float32).reshape(1, 64)
    lk1 = np.asarray(inputs["L0_attn_lambda_k1"], dtype=np.float32).reshape(1, 64)
    lq2 = np.asarray(inputs["L0_attn_lambda_q2"], dtype=np.float32).reshape(1, 64)
    lk2 = np.asarray(inputs["L0_attn_lambda_k2"], dtype=np.float32).reshape(1, 64)
    sw = np.asarray(inputs["L0_attn_subln_weight"], dtype=np.float32).reshape(1, 128)
    fb = np.asarray(inputs["L0_ffn_bias"], dtype=np.float32).reshape(1, D)
    fc = np.asarray(inputs["L0_ffn_factors"], dtype=np.float32)
    in_maps = []
    for c in range(8):
        b, gg = c // 2, c % 2
        sl = slice(gg * 1024, (gg + 1) * 1024)
        in_maps.append({
            "x_full": np.ascontiguousarray(x[b]),
            "x_mine": np.ascontiguousarray(x[b, gg * TM:(gg + 1) * TM]),
            "wqt": np.ascontiguousarray(Wq[sl].T),
            "wkt": np.ascontiguousarray(Wk[sl].T),
            "wvt": np.ascontiguousarray(Wv[sl].T),
            "wot": np.ascontiguousarray(Wo[:, sl].T),
            "lq1": lq1, "lk1": lk1, "lq2": lq2, "lk2": lk2,
            "subln": sw, "fbias": fb,
            "fct": np.ascontiguousarray(fc[c]),
        })
    return in_maps


def kernel(**inputs):
    nc = _get_nc()
    in_maps = make_in_maps(inputs)
    res = run_bass_kernel_spmd(nc, in_maps, core_ids=list(range(8)))
    out = np.empty((4, TB, D), dtype=np.float32)
    for c in range(8):
        b, gg = c // 2, c % 2
        out[b, gg * TM:(gg + 1) * TM] = res.results[c]["out"]
    return out


# version 16
# speedup vs baseline: 1.0031x; 1.0031x over previous
import math
from contextlib import ExitStack

import numpy as np

import concourse.bass as bass
import concourse.mybir as mybir
from concourse.bass_utils import run_bass_kernel_spmd

F32 = mybir.dt.float32
BF16 = mybir.dt.bfloat16
AF = mybir.ActivationFunctionType
ALU = mybir.AluOpType
AX = mybir.AxisListType

D = 2048
TB = 2048
TM = 1024
NH = 8
LAM_INIT = 0.8 - 0.6 * math.exp(-0.3 * 12)
INV_SQRT8 = 1.0 / math.sqrt(8.0)
EPS = 1e-5

ENGS = ("sync", "gpsimd", "vector", "scalar", "tensor")
NDMASEM = 6


class _G:

    def __init__(self):
        self.seq = {e: [] for e in ENGS}
        self.ndma = {e: 0 for e in ENGS}
        self.ncc = 0

    def op(self, eng, emit, deps=()):
        i = len(self.seq[eng])
        self.seq[eng].append({"emit": emit, "deps": list(deps), "kind": "c",
                              "inc": False})
        return ("e", eng, i)

    def dma(self, eng, emit, deps=()):
        di = self.ndma[eng]
        deps = list(deps)
        if di >= NDMASEM:
            deps.append(("d", eng, di - NDMASEM))
        self.seq[eng].append({"emit": emit, "deps": deps, "kind": "d",
                              "di": di})
        self.ndma[eng] += 1
        return ("d", eng, di)

    def coll(self, emit, deps=()):
        ci = self.ncc
        self.ncc += 1
        self.seq["gpsimd"].append({"emit": emit, "deps": list(deps),
                                   "kind": "cc", "ci": ci})
        return ("cc", ci)

    def finalize_and_emit(self, nc, stack):
        for e in ENGS:
            for o in self.seq[e]:
                for dep in o["deps"]:
                    if dep[0] == "e":
                        self.seq[dep[1]][dep[2]]["inc"] = True
        inc_at = {}
        for e in ENGS:
            n = 0
            for i, o in enumerate(self.seq[e]):
                if o["kind"] == "c" and o["inc"]:
                    n += 1
                inc_at[(e, i)] = n
        esem = {e: stack.enter_context(nc.semaphore(f"se_{e}")) for e in ENGS}
        dsem = {e: [stack.enter_context(nc.semaphore(f"sd_{e}_{k}"))
                    for k in range(NDMASEM)]
                for e in ENGS if self.ndma[e] > 0}
        csem = [stack.enter_context(nc.semaphore(f"scc_{j}"))
                for j in range(self.ncc)]

        def resolve(dep):
            if dep[0] == "e":
                _, e, i = dep
                assert self.seq[e][i]["inc"]
                return (("e", e), esem[e], inc_at[(e, i)])
            if dep[0] == "d":
                _, e, di = dep
                return (("d", e, di % NDMASEM), dsem[e][di % NDMASEM],
                        16 * (di // NDMASEM + 1))
            _, ci = dep
            return (("cc", ci), csem[ci], 1)

        block = stack.enter_context(nc.Block())

        def run_engine(engobj, ename):
            waited = {}
            for o in self.seq[ename]:
                for dep in o["deps"]:
                    key, sem, val = resolve(dep)
                    if waited.get(key, 0) < val:
                        engobj.wait_ge(sem, val)
                        waited[key] = val
                inst = o["emit"](engobj)
                if o["kind"] == "d":
                    inst.then_inc(dsem[ename][o["di"] % NDMASEM], 16)
                elif o["kind"] == "cc":
                    inst.then_inc(csem[o["ci"]], 1)
                elif o["inc"]:
                    inst.then_inc(esem[ename], 1)

        block.sync(lambda e: run_engine(e, "sync"))
        block.gpsimd(lambda e: run_engine(e, "gpsimd"))
        block.vector(lambda e: run_engine(e, "vector"))
        block.scalar(lambda e: run_engine(e, "scalar"))
        block.tensor(lambda e: run_engine(e, "tensor"))


def build_nc():
    nc = bass.Bass("TRN2", target_bir_lowering=False, debug=False)
    g = _G()

    x_full = nc.dram_tensor("x_full", [TB, D], F32, kind="ExternalInput").ap()
    x_mine = nc.dram_tensor("x_mine", [TM, D], F32, kind="ExternalInput").ap()
    wqt = nc.dram_tensor("wqt", [D, 1024], F32, kind="ExternalInput").ap()
    wkt = nc.dram_tensor("wkt", [D, 1024], F32, kind="ExternalInput").ap()
    wvt = nc.dram_tensor("wvt", [D, 1024], F32, kind="ExternalInput").ap()
    wot = nc.dram_tensor("wot", [1024, D], F32, kind="ExternalInput").ap()
    lq1 = nc.dram_tensor("lq1", [1, 64], F32, kind="ExternalInput").ap()
    lk1 = nc.dram_tensor("lk1", [1, 64], F32, kind="ExternalInput").ap()
    lq2 = nc.dram_tensor("lq2", [1, 64], F32, kind="ExternalInput").ap()
    lk2 = nc.dram_tensor("lk2", [1, 64], F32, kind="ExternalInput").ap()
    subln = nc.dram_tensor("subln", [1, 128], F32, kind="ExternalInput").ap()
    fbias = nc.dram_tensor("fbias", [1, D], F32, kind="ExternalInput").ap()
    fct = nc.dram_tensor("fct", [D, D], F32, kind="ExternalInput").ap()
    out_e = nc.dram_tensor("out", [TM, D], F32, kind="ExternalOutput").ap()

    xnm_dram = nc.dram_tensor("xnm_dram", [TM, D], BF16).ap()
    q_dram = nc.dram_tensor("q_dram", [NH, 128, TB], BF16).ap()
    k_dram = nc.dram_tensor("k_dram", [NH, 128, TB], BF16).ap()
    attn_dram = nc.dram_tensor("attn_dram", [NH, TB, 128], BF16).ap()
    fct_b = nc.dram_tensor("fct_b", [D, D], F32).ap()
    f_sum = nc.dram_tensor("f_sum", [D, D], F32, addr_space="Shared").ap()
    f_bf = nc.dram_tensor("f_bf", [D, D], BF16).ap()
    rs_in = [nc.dram_tensor(f"rs_in{h}", [TB, 1024], BF16).ap() for h in (0, 1)]
    outA = nc.dram_tensor("outA", [TM, D], F32).ap()
    rs_out = [nc.dram_tensor(f"rs_out{h}", [TM, 1024], BF16).ap() for h in (0, 1)]

    SB = 16512
    TOP = 229344

    def sb(name, shape, dtype, off):
        nbytes = shape[1] * mybir.dt.size(dtype)
        assert off + nbytes <= TOP, (name, off, nbytes)
        return nc.alloc_sbuf_tensor_at(name, shape, dtype, offset=off).ap()

    class Bump:
        def __init__(self, base):
            self.off = base

        def __call__(self, name, shape, dtype):
            nbytes = shape[1] * mybir.dt.size(dtype)
            off = (self.off + 31) // 32 * 32
            self.off = off + nbytes
            return sb(name, shape, dtype, off)

    pz = Bump(SB)
    ones_f = pz("ones_f", [1, 128], F32)
    sub_row = pz("sub_row", [1, 128], F32)
    lamv = pz("lamv", [1, 16], F32)
    lpr1 = pz("lpr1", [1, 64], F32)
    lpr2 = pz("lpr2", [1, 64], F32)
    lnb = pz("lnb", [128, 1], F32)
    swb = pz("swb", [128, 128], F32)
    _fb_off = (pz.off + 31) // 32 * 32
    fb_row = pz("fb_row", [1, D], F32)
    fb_b = pz("fb_b", [128, D], F32)
    snm = pz("snm", [128, 24], F32)
    sss = pz("sss", [128, 24], F32)
    sm2 = pz("sm2", [128, 24], F32)
    svar = pz("svar", [128, 24], F32)
    srst = pz("srst", [128, 24], F32)
    er1 = pz("er1", [128, 4], F32)
    er2 = pz("er2", [128, 4], F32)
    elr = pz("elr", [128, 4], F32)
    ess = pz("ess", [128, 4], F32)
    elnv = pz("elnv", [128, 4], F32)
    ersr = pz("ersr", [128, 4], F32)
    eu1 = [pz(f"eu1_{p}", [128, 128], F32) for p in (0, 1)]
    eac = [pz(f"eac_{p}", [128, 128], F32) for p in (0, 1)]
    esq = sb("esq", [128, 128], F32, _fb_off)
    epsb = pz("epsb", [128, 1], F32)
    idt = pz("idt", [128, 128], F32)

    A_ = (pz.off + 31) // 32 * 32
    B_ = A_ + 65536
    C_ = B_ + 32768
    D_ = C_ + 33280
    S_ = D_ + 32768
    assert S_ + 26624 <= TOP, (S_, TOP)

    xnT_all = sb("xnT_all", [128, 16 * TB], BF16, A_)
    xnT = [xnT_all[:, dc * TB:(dc + 1) * TB] for dc in range(16)]
    ptil = [sb(f"ptil_{p}", [128, 16 * 1024], BF16, A_ + p * 32768) for p in (0, 1)]
    F_sb = [sb(f"F_{dc}", [128, D], BF16, A_ + dc * 4096) for dc in range(16)]
    wtb = [sb(f"wtb_{s}", [128, 16 * 512], BF16, B_ + s * 16384) for s in (0, 1)]
    woT = [sb(f"woT_{h}", [128, D], BF16, B_ + h * 4096) for h in range(NH)]

    v_sb = [sb(f"v_{sc}", [128, 1032], BF16, C_ + sc * 2080) for sc in range(16)]
    acT = [sb(f"acT_{h}", [128, TB], BF16, C_ + h * 4096) for h in range(NH)]
    ob_bf = [sb(f"ob_{p}", [128, D], F32, C_ + p * 8192) for p in (0, 1)]
    rso_t = [[sb(f"rso{ab}_{p}", [128, 1024], BF16,
                 C_ + 16384 + (ab * 2 + p) * 2048) for p in (0, 1)]
             for ab in (0, 1)]
    xb_t = [sb(f"xb_{p}", [128, D], F32, D_ + p * 8192) for p in (0, 1)]
    xs_s2 = [sb("xs_s", [128, D], F32, D_ + 16384),
             sb("xs_sb", [128, D], F32, C_)]
    sq_scr = sb("sq_scr", [128, D], BF16, D_ + 24576)
    fstg = [sb(f"fstg_{p}", [128, D], F32, D_ + 16384 + p * 8192) for p in (0, 1)]
    rs_stage = [sb(f"rsst_{p}", [128, 1024], BF16, D_ + p * 2048) for p in (0, 1)]
    eu1 += [sb(f"eu1_{p}", [128, 128], F32, D_ + (p - 2) * 1024) for p in (2, 3)]
    eac = eac + [sb(f"eac_{p}", [128, 128], F32, D_ + (p - 2) * 1024 + 512)
                 for p in (2, 3)]
    osb = [sb(f"osb_{p}", [128, D], F32, D_ + 8192 + p * 8192) for p in (0, 1)]
    qh_t = [sb(f"qh_{p}", [128, TB], BF16, S_ + p * 4096) for p in (0, 1)]
    xmT = ([sb(f"xmT_{k}", [128, TM], BF16, S_ + k * 2048) for k in range(13)]
           + [sb(f"xmT_{k}", [128, TM], BF16, D_ + 4096 + (k - 13) * 2048)
              for k in (13, 14)]
           + [sb("xmT_15", [128, TM], BF16, _fb_off + 512)])
    xs2 = [sb("xs2_0", [128, D], BF16, S_ + 0),
           sb("xs2_1", [128, D], BF16, S_ + 16384)]
    kh_t = [sb(f"kh_{p}", [128, TB], BF16, S_ + 8192 + p * 4096) for p in (0, 1)]
    ab_t = sb("ab_t", [128, 16 * 128], BF16, S_ + 16384)
    st_t = [sb(f"st_{p}", [128, 512], BF16, S_ + 20480 + p * 1024) for p in (0, 1)]
    wst = [sb(f"wst_{p}", [128, 512], F32, S_ + 22528 + p * 2048) for p in (0, 1)]
    sub_sc = sub_row

    pool_h = nc.alloc_psum_tensor("pool", [128, 4096], F32)
    pool = pool_h.ap()

    def bank(i, lo=0, hi=512):
        return pool[:, i * 512 + lo: i * 512 + hi]

    ident_d = nc.inline_tensor(np.eye(128, dtype=np.float32), "ident128").ap()

    xld = [None] * 16

    def emit_xld(i2, deps=()):
        xld[i2] = [g.dma("scalar", (lambda i2, q: lambda e: e.dma_start(
            out=xb_t[i2 % 2][:, q * 512:(q + 1) * 512],
            in_=x_full[i2 * 128:(i2 + 1) * 128, q * 512:(q + 1) * 512]))(i2, q),
            deps=deps) for q in range(4)]

    for i0 in (0, 1):
        emit_xld(i0)
    t_idt = g.dma("sync", lambda e: e.dma_start(out=idt[:, :], in_=ident_d[:, :]))
    t_subrow = g.dma("sync", lambda e: e.dma_start(out=sub_row[:, :], in_=subln[:, :]))
    t_lq1 = g.dma("sync", lambda e: e.dma_start(out=lpr1[:, :], in_=lq1[:, :]))
    t_lk1 = g.dma("sync", lambda e: e.dma_start(out=lpr2[:, :], in_=lk1[:, :]))
    t_fbrow = g.dma("sync", lambda e: e.dma_start(out=fb_row[:, :], in_=fbias[:, :]))

    t_ones = g.op("vector", lambda e: e.memset(ones_f[:, :], 1.0))
    t_eps = g.op("vector", lambda e: e.memset(epsb[:, :], EPS))
    t_m1 = g.op("vector", lambda e: e.tensor_mul(lpr1[:, :], lpr1[:, :], lpr2[:, :]),
                deps=[t_lq1, t_lk1])
    t_s1 = g.op("vector", lambda e: e.reduce_sum(lamv[:, 0:1], lpr1[:, :], axis=AX.X),
                deps=[t_m1])
    t_lq2 = g.dma("sync", lambda e: e.dma_start(out=lpr1[:, :], in_=lq2[:, :]),
                  deps=[t_s1])
    t_lk2 = g.dma("sync", lambda e: e.dma_start(out=lpr2[:, :], in_=lk2[:, :]),
                  deps=[t_s1])
    t_m2 = g.op("vector", lambda e: e.tensor_mul(lpr1[:, :], lpr1[:, :], lpr2[:, :]),
                deps=[t_lq2, t_lk2])
    t_s2 = g.op("vector", lambda e: e.reduce_sum(lamv[:, 1:2], lpr1[:, :], axis=AX.X),
                deps=[t_m2])
    t_e1 = g.op("scalar", lambda e: e.activation(lamv[:, 2:3], lamv[:, 0:1], AF.Exp),
                deps=[t_s1])
    t_e2 = g.op("scalar", lambda e: e.activation(lamv[:, 3:4], lamv[:, 1:2], AF.Exp),
                deps=[t_s2])
    t_ln0 = g.op("vector", lambda e: e.tensor_sub(lamv[:, 4:5], lamv[:, 3:4], lamv[:, 2:3]),
                 deps=[t_e1, t_e2])
    t_ln1 = g.op("vector", lambda e: e.tensor_scalar(
        out=lamv[:, 5:6], in0=lamv[:, 4:5], scalar1=-LAM_INIT, scalar2=None,
        op0=ALU.add), deps=[t_ln0])
    t_subsc = g.op("vector", lambda e: e.tensor_scalar(
        out=sub_sc[:, :], in0=sub_row[:, :], scalar1=(1.0 - LAM_INIT), scalar2=None,
        op0=ALU.mult), deps=[t_subrow])
    t_bsw = g.op("tensor", lambda e: e.matmul(
        bank(7, 0, 128), ones_f[0:1, :], sub_sc[0:1, :], start=True, stop=True),
        deps=[t_ones, t_subsc])
    t_bln = g.op("tensor", lambda e: e.matmul(
        bank(7, 128, 129), ones_f[0:1, :], lamv[0:1, 5:6], start=True, stop=True),
        deps=[t_ones, t_ln1])
    t_esw = g.op("vector", lambda e: e.tensor_copy(swb[:, :], bank(7, 0, 128)),
                 deps=[t_bsw, t_bln])
    t_eln = g.op("vector", lambda e: e.tensor_copy(lnb[:, :], bank(7, 128, 129)),
                 deps=[t_bln])
    t_fbb = []
    prev = None
    for eo in range(4):
        dep = [t_ones, t_fbrow] + ([prev] if prev else [])
        tb_ = g.op("tensor", (lambda eo: lambda e: e.matmul(
            bank(6), ones_f[0:1, :], fb_row[0:1, eo * 512:(eo + 1) * 512],
            start=True, stop=True))(eo), deps=dep)
        te_ = g.op("vector", (lambda eo: lambda e: e.tensor_copy(
            fb_b[:, eo * 512:(eo + 1) * 512], bank(6)))(eo), deps=[tb_])
        t_fbb.append(te_)
        prev = te_

    norm_tok = [None] * 24
    sq_tok = [None] * 24
    store_tok = [None] * 24
    prev_sq = [None]
    prev_psq = [None]
    xs_war = [None, None]
    xm_war = [None, None]
    xnT_ev = {}
    banklast = [None] * 8
    tb_rot = [0]

    ln_st = {}

    def ln_front(i):
        mine = i >= 16
        j = i - 16 if mine else i
        xb = xb_t[i % 2]
        c = i
        if mine:
            war = [norm_tok[i - 2], sq_tok[i - 2]]
            t_ld = [g.dma("sync", (lambda xb, j, q: lambda e: e.dma_start(
                out=xb[:, q * 512:(q + 1) * 512],
                in_=x_mine[j * 128:(j + 1) * 128, q * 512:(q + 1) * 512]))(xb, j, q),
                deps=war) for q in range(4)]
        else:
            t_ld = xld[i]
        ceng = "vector"
        t_sum = g.op(ceng, (lambda xb, c: lambda e: e.reduce_sum(
            snm[:, c:c + 1], xb[:, :], axis=AX.X))(xb, c), deps=t_ld)
        t_nm = g.op(ceng, (lambda c: lambda e: e.tensor_scalar(
            out=snm[:, c:c + 1], in0=snm[:, c:c + 1], scalar1=-1.0 / D,
            scalar2=None, op0=ALU.mult))(c), deps=[t_sum])
        sqdeps = list(t_ld) + ([prev_sq[0]] if prev_sq[0] else [])
        t_sq = g.op("scalar", (lambda xb, c: lambda e: e.activation(
            sq_scr[:, :], xb[:, :], AF.Square, accum_out=sss[:, c:c + 1]))(xb, c),
            deps=sqdeps)
        prev_sq[0] = t_sq
        sq_tok[i] = t_sq
        t_mm2 = g.op(ceng, (lambda c: lambda e: e.tensor_scalar(
            out=sm2[:, c:c + 1], in0=snm[:, c:c + 1], scalar1=snm[:, c:c + 1],
            scalar2=None, op0=ALU.mult))(c), deps=[t_nm])
        t_var = g.op(ceng, (lambda c: lambda e: e.scalar_tensor_tensor(
            out=svar[:, c:c + 1], in0=sss[:, c:c + 1], scalar=1.0 / D,
            in1=sm2[:, c:c + 1], op0=ALU.mult, op1=ALU.subtract))(c),
            deps=[t_sq, t_mm2])
        t_lnv = g.op("scalar", (lambda c: lambda e: e.activation(
            svar[:, c:c + 1], svar[:, c:c + 1], AF.Ln, bias=epsb[:, :]))(c),
            deps=[t_var, t_eps])
        t_rst = g.op("scalar", (lambda c: lambda e: e.activation(
            srst[:, c:c + 1], svar[:, c:c + 1], AF.Exp, scale=-0.5))(c), deps=[t_lnv])
        if mine:
            t_rst = g.op("vector", (lambda c: lambda e: e.tensor_scalar(
                out=srst[:, c:c + 1], in0=srst[:, c:c + 1], scalar1=INV_SQRT8,
                scalar2=None, op0=ALU.mult))(c), deps=[t_rst])
        ln_st[i] = (t_nm, t_rst)

    def ln_back(i):
        mine = i >= 16
        j = i - 16 if mine else i
        xb = xb_t[i % 2]
        c = i
        ceng = "vector"
        t_nm, t_rst = ln_st[i]
        t_sq = sq_tok[i]
        if mine:
            xso = xs2[j % 2]
            ndeps = [t_nm, t_rst]
            if xm_war[j % 2] is not None:
                ndeps.append(xm_war[j % 2])
            t_no = g.op(ceng, (lambda xb, c, xso: lambda e: e.tensor_scalar(
                out=xso[:, :], in0=xb[:, :], scalar1=snm[:, c:c + 1],
                scalar2=srst[:, c:c + 1], op0=ALU.add, op1=ALU.mult))(xb, c, xso),
                deps=ndeps)
            norm_tok[i] = t_no
            t_st = g.dma("sync", (lambda j, xso: lambda e: e.dma_start(
                out=xnm_dram[j * 128:(j + 1) * 128, :], in_=xso[:, :]))(j, xso),
                deps=[t_no])
            store_tok[i] = t_st
            xm_war[j % 2] = t_st
            return
        xp = i % 2
        xs_s = xs_s2[xp]
        ndeps = [t_nm, t_rst]
        if xs_war[xp] is not None:
            ndeps.append(xs_war[xp])
        t_no = g.op(ceng, (lambda xb, c, xs_s: lambda e: e.tensor_scalar(
            out=xs_s[:, :], in0=xb[:, :], scalar1=snm[:, c:c + 1],
            scalar2=srst[:, c:c + 1], op0=ALU.add, op1=ALU.mult))(xb, c, xs_s),
            deps=ndeps)
        norm_tok[i] = t_no
        if i + 2 < 16:
            emit_xld(i + 2, deps=[t_no, t_sq])
        if True:
            last_tr = None
            for dcg in range(4):
                base = ((i * 4 + dcg) % 2) * 4
                trs = []
                for k in range(4):
                    dcb = dcg * 4 + k
                    tb = base + k
                    tdeps = [t_no, t_idt]
                    if banklast[tb] is not None:
                        tdeps.append(banklast[tb])
                    t_tr = g.op("tensor", (lambda dcb, tb, xs_s: lambda e: e.transpose(
                        bank(tb, 0, 128), xs_s[:, dcb * 128:(dcb + 1) * 128],
                        idt[:, :]))(dcb, tb, xs_s), deps=tdeps)
                    trs.append(t_tr)
                    last_tr = t_tr
                t_evt = g.op("scalar", (lambda dcg, i, base: lambda e: e.copy(
                    xnT_all.rearrange("p (dcb c) -> p dcb c", c=TB)
                        [:, dcg * 4:(dcg + 1) * 4, i * 128:(i + 1) * 128],
                    pool.rearrange("p (b c) -> p b c", c=512)
                        [:, base:base + 4, 0:128]))(dcg, i, base),
                    deps=trs)
                for k in range(4):
                    banklast[base + k] = t_evt
                    xnT_ev[(i, dcg * 4 + k)] = t_evt
            xs_war[xp] = last_tr

    wsrc = (wqt, wkt, wvt)
    wt_cast = {}
    wt_last_reader = [None, None]
    wst_last = [None, None]

    def w_slice(si):
        p, eh = si // 2, si % 2
        slot = si % 2
        casts = []
        for dc in range(16):
            ldeps = []
            if wst_last[dc % 2] is not None:
                ldeps.append(wst_last[dc % 2])
            t_l = g.dma("sync", (lambda p, eh, dc: lambda e: e.dma_start(
                out=wst[dc % 2][:, :],
                in_=wsrc[p][dc * 128:(dc + 1) * 128, eh * 512:(eh + 1) * 512]))(p, eh, dc),
                deps=ldeps)
            t_c = g.op("gpsimd", (lambda slot, dc: lambda e: e.tensor_copy(
                wtb[slot][:, dc * 512:(dc + 1) * 512], wst[dc % 2][:, :]))(slot, dc),
                deps=[t_l])
            wst_last[dc % 2] = t_c
            casts.append(t_c)
        wt_cast[si] = casts

    ln_front(0)
    for i in range(24):
        if i + 1 < 24:
            ln_front(i + 1)
        ln_back(i)
        if i == 3:
            w_slice(0)
        elif i == 7:
            w_slice(1)

    t_vmem = []
    for sc in range(16):
        deps = [xs_war[1]] if sc < 5 else []
        t_vmem.append(g.op("vector", (lambda sc: lambda e: e.memset(
            v_sb[sc][:, :], 1.0))(sc), deps=deps))

    w_slice(2)
    w_slice(3)
    w_slice(4)
    w_slice(5)

    t_fstage = g.dma("gpsimd", lambda e: e.dma_start(out=fct_b[:, :], in_=fct[:, :]))
    t_ar = g.coll(lambda e: e.collective_compute(
        "AllReduce", ALU.add, replica_groups=[list(range(8))],
        ins=[fct_b[:, :]], outs=[f_sum[:, :]]), deps=[t_fstage])

    q_store = {}
    k_store = {}
    v_evac = []
    slice_last_mm = {}
    if banklast[6] is None:
        banklast[6] = t_fbb[-1]
    if banklast[7] is None:
        banklast[7] = t_eln
    stage_last = [None, None]

    for si in range(6):
        p, eh = si // 2, si % 2
        slot = si % 2
        tws = wt_cast[si]
        if p < 2:
            for ts in range(4):
                for ec in range(4):
                    b = (ts % 2) * 4 + ec
                    head = eh * 4 + ec
                    grp_deps = tws + [xnT_ev[(ts * 4 + k, dc)]
                                      for k in range(4) for dc in range(16)]
                    if banklast[b] is not None:
                        grp_deps = grp_deps + [banklast[b]]
                    last_mm = None
                    for dc in range(16):
                        tok = g.op("tensor", (lambda slot, dc, ec, ts, b: lambda e:
                            e.matmul(bank(b),
                                     wtb[slot][:, dc * 512 + ec * 128: dc * 512 + (ec + 1) * 128],
                                     xnT[dc][:, ts * 512:(ts + 1) * 512],
                                     start=(dc == 0), stop=(dc == 15)))(slot, dc, ec, ts, b),
                            deps=(grp_deps if dc == 0 else []))
                        last_mm = tok
                    wt_last_reader[slot] = last_mm
                    slice_last_mm[si] = last_mm
                    sb_i = (ts * 4 + ec) % 2
                    ev_deps = [last_mm]
                    if stage_last[sb_i] is not None:
                        ev_deps.append(stage_last[sb_i])
                    t_ev = g.op("vector", (lambda sb_i, b: lambda e: e.tensor_copy(
                        st_t[sb_i][:, :], bank(b)))(sb_i, b), deps=ev_deps)
                    banklast[b] = t_ev
                    dstd = q_dram if p == 0 else k_dram
                    t_sto = g.dma("scalar", (lambda sb_i, dstd, head, ts: lambda e:
                        e.dma_start(out=dstd[head, :, ts * 512:(ts + 1) * 512],
                                    in_=st_t[sb_i][:, :]))(sb_i, dstd, head, ts),
                        deps=[t_ev])
                    stage_last[sb_i] = t_sto
                    store = q_store if p == 0 else k_store
                    store.setdefault(head, []).append(t_sto)
        else:
            for sc in range(16):
                b = sc % 8
                grp_deps = tws + [xnT_ev[((sc // 4) * 4 + k, dc)]
                                  for k in range(4) for dc in range(16)]
                if banklast[b] is not None:
                    grp_deps = grp_deps + [banklast[b]]
                last_mm = None
                for dc in range(16):
                    tok = g.op("tensor", (lambda slot, dc, sc, b: lambda e:
                        e.matmul(bank(b),
                                 xnT[dc][:, sc * 128:(sc + 1) * 128],
                                 wtb[slot][:, dc * 512:(dc + 1) * 512],
                                 start=(dc == 0), stop=(dc == 15)))(slot, dc, sc, b),
                        deps=(grp_deps if dc == 0 else []))
                    last_mm = tok
                wt_last_reader[slot] = last_mm
                slice_last_mm[si] = last_mm
                t_ev = g.op("vector", (lambda sc, b, eh: lambda e: e.tensor_copy(
                    v_sb[sc][:, eh * 516: eh * 516 + 516]
                        .rearrange("p (h q) -> p h q", q=129)[:, :, 0:128],
                    bank(b).rearrange("p (h q) -> p h q", q=128)))(sc, b, eh),
                    deps=[last_mm, t_vmem[sc]])
                banklast[b] = t_ev
                v_evac.append(t_ev)

    for si in range(2, 6):
        first = wt_cast[si][0]
        g.seq[first[1]][first[2]]["deps"].append(slice_last_mm[si - 2])

    qk_tok = {}
    qh_last = [None, None]
    kh_last = [None, None]

    def load_head(h, extra=()):
        hp = h % 2
        qdeps = q_store[h] + list(extra) + ([qh_last[hp]] if qh_last[hp] else [])
        t_qh = g.dma("sync", (lambda h, hp: lambda e: e.dma_start(
            out=qh_t[hp][:, :], in_=q_dram[h, :, :]))(h, hp), deps=qdeps)
        kdeps = k_store[h] + ([kh_last[hp]] if kh_last[hp] else [])
        t_kh = g.dma("sync", (lambda h, hp: lambda e: e.dma_start(
            out=kh_t[hp][:, :], in_=k_dram[h, :, :]))(h, hp), deps=kdeps)
        qk_tok[h] = (t_qh, t_kh)

    load_head(0, extra=[store_tok[22]])
    load_head(1)

    a_free = [wt_last_reader[0], wt_last_reader[1]]
    ab_free = [store_tok[23]]
    ab_last = [None]
    epi_last = [None, None, None, None]
    ptil_last = [None, None]
    attn_store = [None] * NH
    sq_free = [t_fbb[-1]]
    units = [(h, tt) for h in range(NH) for tt in range(4)]
    score_info = {}

    def emit_score_group(ui, j):
        h, tt = units[ui]
        S = 4 * (ui % 2)
        if j == 0 and tt == 0 and h + 1 < NH and (h + 1) not in qk_tok:
            load_head(h + 1)
        hp = h % 2
        t_qh, t_kh = qk_tok[h]
        out = []
        for sc in (2 * j, 2 * j + 1):
            b1 = S + 2 * (sc % 2)
            b2 = b1 + 1
            mm_deps1 = [t_qh, t_kh]
            if banklast[b1] is not None:
                mm_deps1 = mm_deps1 + [banklast[b1]]
            t_s1m = g.op("tensor", (lambda hp, sc, tt, b1, ui=ui: lambda e: e.matmul(
                bank(b1, 0, 512),
                kh_t[hp][0:64, sc * 128:(sc + 1) * 128],
                qh_t[hp][0:64, tt * 512:(tt + 1) * 512],
                start=True, stop=True, tile_position=(0, 0)).annotate(
                f"s1m_u{ui}_sc{sc}"))(hp, sc, tt, b1),
                deps=mm_deps1)
            mm_deps2 = []
            if banklast[b2] is not None:
                mm_deps2 = [banklast[b2]]
            t_s2m = g.op("tensor", (lambda hp, sc, tt, b2: lambda e: e.matmul(
                bank(b2, 0, 512),
                kh_t[hp][64:128, sc * 128:(sc + 1) * 128],
                qh_t[hp][64:128, tt * 512:(tt + 1) * 512],
                start=True, stop=True, tile_position=(64, 0)))(hp, sc, tt, b2),
                deps=mm_deps2)
            out.append((t_s1m, t_s2m, b1))
            if tt == 3 and sc == 15:
                qh_last[hp] = t_s2m
                kh_last[hp] = t_s2m
        return out

    def emit_exps(ui, mms):
        par = ui % 2
        exp_toks = []
        for sc in range(16):
            t_s1m, t_s2m, b1 = mms[sc]
            edeps = [t_s1m, t_s2m]
            if ptil_last[par] is not None:
                edeps.append(ptil_last[par])
            if a_free:
                edeps = edeps + a_free
            t_ex = g.op("scalar", (lambda par, sc, b1, ui=ui: lambda e: e.activation(
                out=ptil[par][:, sc * 1024:(sc + 1) * 1024]
                    .rearrange("p (b c) -> p b c", c=512),
                in_=pool[:, b1 * 512:(b1 + 2) * 512]
                    .rearrange("p (b c) -> p b c", c=512),
                func=AF.Exp, scale=0.125).annotate(f"exp_u{ui}_sc{sc}"))(par, sc, b1),
                deps=edeps)
            exp_toks.append(t_ex)
            banklast[b1] = t_ex
            banklast[b1 + 1] = t_ex
            if sc + 2 < 16:
                nxt = mms[sc + 2][0]
                g.seq[nxt[1]][nxt[2]]["deps"].append(t_ex)
        if a_free:
            a_free.clear()
        score_info[ui] = exp_toks

    def emit_av_chunk(ui, tsub, half, av_first):
        h, tt = units[ui]
        par = ui % 2
        S = 4 * (ui % 2)
        a1 = S + (tsub % 2) * 2
        a2 = a1 + 1
        last = None
        for sc in range(half * 8, half * 8 + 8):
            deps1 = []
            deps2 = []
            if sc == 0:
                deps1 = list(score_info[ui])
                for btok in (banklast[a1], banklast[a2]):
                    if btok is not None:
                        deps1.append(btok)
                deps1 += v_evac
                if banklast[a2] is not None:
                    deps2 = [banklast[a2]]
            tok1 = g.op("tensor", (lambda par, sc, tsub, h, a1: lambda e:
                e.matmul(bank(a1, 0, 129),
                         ptil[par][:, sc * 1024 + tsub * 128: sc * 1024 + (tsub + 1) * 128],
                         v_sb[sc][:, h * 129: (h + 1) * 129],
                         start=(sc == 0), stop=(sc == 15)))(par, sc, tsub, h, a1),
                deps=deps1)
            tok2 = g.op("tensor", (lambda par, sc, tsub, h, a2: lambda e:
                e.matmul(bank(a2, 0, 129),
                         ptil[par][:, sc * 1024 + 512 + tsub * 128: sc * 1024 + 512 + (tsub + 1) * 128],
                         v_sb[sc][:, h * 129: (h + 1) * 129],
                         start=(sc == 0), stop=(sc == 15)))(par, sc, tsub, h, a2),
                deps=deps2)
            last = (tok1, tok2)
        return last

    def emit_dve_chain(ui, tsub, last_mms):
        par = ui % 2
        S = 4 * (ui % 2)
        a1 = S + (tsub % 2) * 2
        a2 = a1 + 1
        last_a1, last_a2 = last_mms
        ptil_last[par] = last_a2
        edeps = [last_a1, last_a2, t_eln]
        if epi_last[tsub] is not None:
            edeps.append(epi_last[tsub])
        t_r1 = g.op("vector", (lambda tsub, a1: lambda e: e.reciprocal(
            er1[:, tsub:tsub + 1], bank(a1, 128, 129)))(tsub, a1), deps=edeps)
        t_r2 = g.op("vector", (lambda tsub, a2: lambda e: e.reciprocal(
            er2[:, tsub:tsub + 1], bank(a2, 128, 129)))(tsub, a2), deps=[t_r1])
        t_lr = g.op("vector", (lambda tsub: lambda e: e.tensor_mul(
            elr[:, tsub:tsub + 1], er2[:, tsub:tsub + 1], lnb[:, :]))(tsub),
            deps=[t_r2])
        t_u1 = g.op("vector", (lambda tsub, a1: lambda e: e.tensor_scalar(
            out=eu1[tsub][:, :], in0=bank(a1, 0, 128),
            scalar1=er1[:, tsub:tsub + 1], scalar2=None, op0=ALU.mult))(tsub, a1),
            deps=[t_lr])
        t_ac = g.op("vector", (lambda tsub, a2: lambda e: e.scalar_tensor_tensor(
            out=eac[tsub][:, :], in0=bank(a2, 0, 128),
            scalar=elr[:, tsub:tsub + 1], in1=eu1[tsub][:, :],
            op0=ALU.mult, op1=ALU.add))(tsub, a2), deps=[t_u1])
        banklast[a1] = t_ac
        banklast[a2] = t_ac
        ssdeps = [t_ac]
        if sq_free[0] is not None:
            ssdeps.append(sq_free[0])
            sq_free[0] = None
        t_ss = g.op("vector", (lambda tsub: lambda e: e.scalar_tensor_tensor(
            out=esq[:, :], in0=eac[tsub][:, :], scalar=1.0, in1=eac[tsub][:, :],
            op0=ALU.mult, op1=ALU.mult, accum_out=ess[:, tsub:tsub + 1]))(tsub),
            deps=ssdeps)
        return t_ss, t_ac

    def emit_epi_act(ui, ss_toks):
        t_lv = g.op("scalar", lambda e: e.activation(
            elnv[:, :], ess[:, :], AF.Ln, scale=1.0 / 128.0, bias=epsb[:, :]),
            deps=ss_toks + [t_eps])
        t_rr = g.op("scalar", lambda e: e.activation(
            ersr[:, :], elnv[:, :], AF.Exp, scale=-0.5), deps=[t_lv])
        return t_rr

    def emit_fins(ui, t_rr, ac_info):
        h, tt = units[ui]
        head_finals = []
        for tsub in range(4):
            abdeps = [t_rr, t_esw, ac_info[tsub]]
            if ab_last[0] is not None and tt == 0 and tsub == 0:
                abdeps.append(ab_last[0])
            if ab_free:
                abdeps += ab_free
                ab_free.clear()
            t_fin = g.op("vector", (lambda tsub, tt: lambda e:
                e.scalar_tensor_tensor(
                    out=ab_t[:, (tt * 4 + tsub) * 128:(tt * 4 + tsub + 1) * 128],
                    in0=eac[tsub][:, :], scalar=ersr[:, tsub:tsub + 1], in1=swb[:, :],
                    op0=ALU.mult, op1=ALU.mult))(tsub, tt), deps=abdeps)
            epi_last[tsub] = t_fin
            head_finals.append(t_fin)
        if tt == 3:
            t_asto = g.dma("scalar", (lambda h: lambda e: e.dma_start(
                out=attn_dram[h].rearrange("(c p) e -> p c e", p=128),
                in_=ab_t[:, :].rearrange("p (c e) -> p c e", e=128)))(h),
                deps=head_finals)
            attn_store[h] = t_asto
            ab_last[0] = t_asto

    mms0 = []
    for j in range(8):
        mms0 += emit_score_group(0, j)
    emit_exps(0, mms0)
    for ui in range(len(units)):
        nxt = ui + 1 < len(units)
        mms = []
        ss_toks = []
        ac_info = []
        for j in range(8):
            if nxt:
                mms += emit_score_group(ui + 1, j)
            tsub, half = j // 2, j % 2
            last_mms = emit_av_chunk(ui, tsub, half, av_first=(j == 0))
            if half == 1:
                t_ss, t_ac = emit_dve_chain(ui, tsub, last_mms)
                ss_toks.append(t_ss)
                ac_info.append(t_ac)
        score_info.pop(ui)
        if nxt:
            emit_exps(ui + 1, mms)
        t_rr = emit_epi_act(ui, ss_toks)
        emit_fins(ui, t_rr, ac_info)

    c_free = [t for t in ptil_last if t is not None]
    fstg_last = [None, None]
    fstg_frees = [[xs_war[0]], [sq_tok[23]]]
    t_woc = []
    for h in range(NH):
        fp = h % 2
        ldeps = list(fstg_frees[fp])
        fstg_frees[fp] = []
        if fstg_last[fp] is not None:
            ldeps.append(fstg_last[fp])
        t_l = g.dma("sync", (lambda h, fp: lambda e: e.dma_start(
            out=fstg[fp][:, :], in_=wot[h * 128:(h + 1) * 128, :]))(h, fp),
            deps=ldeps)
        t_c = g.op("gpsimd", (lambda h, fp: lambda e: e.tensor_copy(
            woT[h][:, :], fstg[fp][:, :]))(h, fp),
            deps=[t_l, wt_last_reader[h % 2]])
        fstg_last[fp] = t_c
        t_woc.append(t_c)

    t_acT = []
    for h in range(NH):
        deps = [attn_store[h]] + c_free
        t_acT.append(g.dma("sync", (lambda h: lambda e: e.dma_start(
            out=acT[h][:, :], in_=attn_dram[h], transpose=True))(h), deps=deps))

    t_fc = g.dma("gpsimd", lambda e: e.dma_start(out=f_bf[:, :], in_=f_sum[:, :]),
                 deps=[t_ar])
    af_deps = [t for t in ptil_last if t is not None]
    t_F = []
    for dc in range(16):
        t_F.append(g.dma("sync", (lambda dc: lambda e: e.dma_start(
            out=F_sb[dc][:, :], in_=f_bf[dc * 128:(dc + 1) * 128, :]))(dc),
            deps=[t_fc] + af_deps))

    rs_tok = [None, None]
    rstage_last = [None, None]
    rs_store_tok = [[], []]

    xmT_frees = {0: [qh_last[0]], 1: [qh_last[0]], 2: [qh_last[1]],
                 3: [qh_last[1]], 4: [kh_last[0]], 5: [kh_last[0]],
                 6: [kh_last[1]], 7: [kh_last[1]], 8: [ab_last[0]],
                 9: [ab_last[0]], 10: [stage_last[0], stage_last[1]],
                 11: [wst_last[0]], 12: [wst_last[1]],
                 13: [norm_tok[22], sq_tok[22]], 14: [norm_tok[22], sq_tok[22]],
                 15: [t_fbb[-1]]}
    t_xmT = []
    for dc in range(16):
        t_xmT.append(g.dma("sync", (lambda dc: lambda e: e.dma_start(
            out=xmT[dc][:, :], in_=xnm_dram[:, dc * 128:(dc + 1) * 128],
            transpose=True))(dc),
            deps=[store_tok[16 + k] for k in range(8)] + xmT_frees[dc]))

    def p4_group(eoh, tc):
        tp = tc % 2
        grp = []
        for eo2 in range(2):
            b = tp * 2 + eo2
            deps0 = t_acT + t_woc
            if banklast[b] is not None:
                deps0 = deps0 + [banklast[b]]
            last_mm = None
            for ea in range(NH):
                tok = g.op("tensor", (lambda ea, tc, eoh, eo2, b: lambda e:
                    e.matmul(bank(b),
                             acT[ea][:, tc * 128:(tc + 1) * 128],
                             woT[ea][:, (eoh * 1024 + eo2 * 512):(eoh * 1024 + (eo2 + 1) * 512)],
                             start=(ea == 0), stop=(ea == 7)))(ea, tc, eoh, eo2, b),
                    deps=(deps0 if ea == 0 else []))
                last_mm = tok
            grp.append((last_mm, b))
        evdeps = [t for t, _ in grp]
        if rstage_last[tp] is not None:
            evdeps.append(rstage_last[tp])
        else:
            evdeps += [norm_tok[22], sq_tok[22], epi_last[2], epi_last[3]]
        t_ev0 = g.op("vector", (lambda tp, b0=grp[0][1]: lambda e:
            e.tensor_copy(rs_stage[tp][:, 0:512], bank(b0)))(tp), deps=evdeps)
        t_ev1 = g.op("vector", (lambda tp, b1=grp[1][1]: lambda e:
            e.tensor_copy(rs_stage[tp][:, 512:1024], bank(b1)))(tp), deps=[t_ev0])
        banklast[grp[0][1]] = t_ev1
        banklast[grp[1][1]] = t_ev1
        t_rst_ = g.dma("scalar", (lambda tp, eoh, tc: lambda e: e.dma_start(
            out=rs_in[eoh][tc * 128:(tc + 1) * 128, :], in_=rs_stage[tp][:, :]))(tp, eoh, tc),
            deps=[t_ev1])
        rstage_last[tp] = t_rst_
        rs_store_tok[eoh].append(t_rst_)
        return grp

    for tc in range(16):
        grp_tok = p4_group(0, tc)
    rs_tok[0] = g.coll(lambda e: e.collective_compute(
        "ReduceScatter", ALU.add,
        replica_groups=[[0, 1], [2, 3], [4, 5], [6, 7]],
        ins=[rs_in[0][:, :]], outs=[rs_out[0][:, :]]), deps=rs_store_tok[0])

    xm_tok = [None, None]
    oA_store = [None]
    osb0_free = [norm_tok[23], sq_tok[23]]
    fstg_read = [None, None]

    def ffn_combineA(tcf):
        tp = tcf % 2
        xdeps = [fstg_last[tp]]
        if xm_tok[tp] is not None:
            xdeps.append(xm_tok[tp])
        t_xm = g.dma("sync", (lambda tcf, tp: lambda e: e.dma_start(
            out=fstg[tp][:, :], in_=x_mine[tcf * 128:(tcf + 1) * 128, :]))(tcf, tp),
            deps=xdeps)
        mm_last = []
        for eo in range(4):
            b = 4 + eo
            deps0 = t_F + t_xmT
            if banklast[b] is not None:
                deps0 = deps0 + [banklast[b]]
            last_mm = None
            for dc in range(16):
                tok = g.op("tensor", (lambda dc, tcf, eo, b: lambda e: e.matmul(
                    bank(b), xmT[dc][:, tcf * 128:(tcf + 1) * 128],
                    F_sb[dc][:, eo * 512:(eo + 1) * 512],
                    start=(dc == 0), stop=(dc == 15)))(dc, tcf, eo, b),
                    deps=(deps0 if dc == 0 else []))
                last_mm = tok
            mm_last.append((last_mm, b))
        o2s = []
        for eo in range(4):
            last_mm, b = mm_last[eo]
            deps = [last_mm, t_fbb[eo]]
            if oA_store[0] is not None:
                deps.append(oA_store[0])
            else:
                deps += osb0_free
            t_o1 = g.op("vector", (lambda eo, b: lambda e: e.scalar_tensor_tensor(
                out=osb[0][:, eo * 512:(eo + 1) * 512], in0=bank(b), scalar=1.0,
                in1=fb_b[:, eo * 512:(eo + 1) * 512], op0=ALU.mult, op1=ALU.add))(eo, b),
                deps=deps)
            banklast[b] = t_o1
            t_o2 = g.op("vector", (lambda tp, eo: lambda e: e.tensor_add(
                osb[0][:, eo * 512:(eo + 1) * 512],
                osb[0][:, eo * 512:(eo + 1) * 512],
                fstg[tp][:, eo * 512:(eo + 1) * 512]))(tp, eo), deps=[t_o1, t_xm])
            o2s.append(t_o2)
        fstg_read[tp] = o2s[-1]
        xm_tok[tp] = o2s[-1]
        t_oA = g.dma("scalar", (lambda tcf: lambda e: e.dma_start(
            out=outA[tcf * 128:(tcf + 1) * 128, :], in_=osb[0][:, :]))(tcf),
            deps=o2s)
        oA_store[0] = t_oA

    for tc in range(16):
        p4_group(1, tc)
        if tc % 2 == 1:
            ffn_combineA(tc // 2)
    rs_tok[1] = g.coll(lambda e: e.collective_compute(
        "ReduceScatter", ALU.add,
        replica_groups=[[0, 1], [2, 3], [4, 5], [6, 7]],
        ins=[rs_in[1][:, :]], outs=[rs_out[1][:, :]]), deps=rs_store_tok[1])
    b_free = [banklast[0], banklast[1]]

    out_store = []
    ob_last = [None, None]
    rso_last = [[None, None], [None, None]]
    osb_last = [oA_store[0], None]
    osb1_free = [fstg_read[0]]
    for tc in range(8):
        tp = tc % 2
        obdeps = [banklast[0], banklast[1], oA_store[0]]
        if ob_last[tp] is not None:
            obdeps.append(ob_last[tp])
        t_ob = g.dma("sync", (lambda tc, tp: lambda e: e.dma_start(
            out=ob_bf[tp][:, :], in_=outA[tc * 128:(tc + 1) * 128, :]))(tc, tp),
            deps=obdeps)
        t_rs = []
        for ab in range(2):
            deps = [rs_tok[ab], banklast[0], banklast[1]]
            if rso_last[ab][tp] is not None:
                deps.append(rso_last[ab][tp])
            t_rs.append(g.dma("sync", (lambda ab, tc, tp: lambda e: e.dma_start(
                out=rso_t[ab][tp][:, :],
                in_=rs_out[ab][tc * 128:(tc + 1) * 128, :]))(ab, tc, tp), deps=deps))
        st_toks = []
        for eo in range(4):
            ab = eo // 2
            deps = [t_ob, t_rs[ab]]
            if osb_last[tp] is not None:
                deps.append(osb_last[tp])
            if tp == 1 and osb1_free:
                deps += osb1_free
                osb1_free.clear()
            t_b = g.op("vector", (lambda tp, eo, ab: lambda e: e.tensor_add(
                osb[tp][:, eo * 512:(eo + 1) * 512],
                ob_bf[tp][:, eo * 512:(eo + 1) * 512],
                rso_t[ab][tp][:, (eo % 2) * 512:(eo % 2 + 1) * 512]))(tp, eo, ab),
                deps=deps)
            st_toks.append(t_b)
        ob_last[tp] = st_toks[-1]
        rso_last[0][tp] = st_toks[1]
        rso_last[1][tp] = st_toks[3]
        t_out = g.dma("scalar", (lambda tp, tc: lambda e: e.dma_start(
            out=out_e[tc * 128:(tc + 1) * 128, :], in_=osb[tp][:, :]))(tp, tc),
            deps=st_toks)
        osb_last[tp] = t_out
        out_store.append(t_out)

    g.op("scalar", lambda e: e.nop(), deps=out_store)

    stack = ExitStack()
    g.finalize_and_emit(nc, stack)
    stack.close()
    return nc


_NC_CACHE = None


def _get_nc():
    global _NC_CACHE
    if _NC_CACHE is None:
        _NC_CACHE = build_nc()
    return _NC_CACHE


def make_in_maps(inputs):
    x = np.ascontiguousarray(np.asarray(inputs["L0_attn_x"], dtype=np.float32))
    Wq = np.asarray(inputs["L0_attn_Wq"], dtype=np.float32)
    Wk = np.asarray(inputs["L0_attn_Wk"], dtype=np.float32)
    Wv = np.asarray(inputs["L0_attn_Wv"], dtype=np.float32)
    Wo = np.asarray(inputs["L0_attn_Wo"], dtype=np.float32)
    lq1 = np.asarray(inputs["L0_attn_lambda_q1"], dtype=np.float32).reshape(1, 64)
    lk1 = np.asarray(inputs["L0_attn_lambda_k1"], dtype=np.float32).reshape(1, 64)
    lq2 = np.asarray(inputs["L0_attn_lambda_q2"], dtype=np.float32).reshape(1, 64)
    lk2 = np.asarray(inputs["L0_attn_lambda_k2"], dtype=np.float32).reshape(1, 64)
    sw = np.asarray(inputs["L0_attn_subln_weight"], dtype=np.float32).reshape(1, 128)
    fb = np.asarray(inputs["L0_ffn_bias"], dtype=np.float32).reshape(1, D)
    fc = np.asarray(inputs["L0_ffn_factors"], dtype=np.float32)
    in_maps = []
    for c in range(8):
        b, gg = c // 2, c % 2
        sl = slice(gg * 1024, (gg + 1) * 1024)
        in_maps.append({
            "x_full": np.ascontiguousarray(x[b]),
            "x_mine": np.ascontiguousarray(x[b, gg * TM:(gg + 1) * TM]),
            "wqt": np.ascontiguousarray(Wq[sl].T),
            "wkt": np.ascontiguousarray(Wk[sl].T),
            "wvt": np.ascontiguousarray(Wv[sl].T),
            "wot": np.ascontiguousarray(Wo[:, sl].T),
            "lq1": lq1, "lk1": lk1, "lq2": lq2, "lk2": lk2,
            "subln": sw, "fbias": fb,
            "fct": np.ascontiguousarray(fc[c]),
        })
    return in_maps


def kernel(**inputs):
    nc = _get_nc()
    in_maps = make_in_maps(inputs)
    res = run_bass_kernel_spmd(nc, in_maps, core_ids=list(range(8)))
    out = np.empty((4, TB, D), dtype=np.float32)
    for c in range(8):
        b, gg = c // 2, c % 2
        out[b, gg * TM:(gg + 1) * TM] = res.results[c]["out"]
    return out


# version 17
# speedup vs baseline: 1.0253x; 1.0222x over previous
import math
from contextlib import ExitStack

import numpy as np

import concourse.bass as bass
import concourse.mybir as mybir
from concourse.bass_utils import run_bass_kernel_spmd

F32 = mybir.dt.float32
BF16 = mybir.dt.bfloat16
AF = mybir.ActivationFunctionType
ALU = mybir.AluOpType
AX = mybir.AxisListType

D = 2048
TB = 2048
TM = 1024
NH = 8
LAM_INIT = 0.8 - 0.6 * math.exp(-0.3 * 12)
INV_SQRT8 = 1.0 / math.sqrt(8.0)
EPS = 1e-5

ENGS = ("sync", "gpsimd", "vector", "scalar", "tensor")
NDMASEM = 6


class _G:

    def __init__(self):
        self.seq = {e: [] for e in ENGS}
        self.ndma = {e: 0 for e in ENGS}
        self.ncc = 0

    def op(self, eng, emit, deps=()):
        i = len(self.seq[eng])
        self.seq[eng].append({"emit": emit, "deps": list(deps), "kind": "c",
                              "inc": False})
        return ("e", eng, i)

    def dma(self, eng, emit, deps=()):
        di = self.ndma[eng]
        deps = list(deps)
        if di >= NDMASEM:
            deps.append(("d", eng, di - NDMASEM))
        self.seq[eng].append({"emit": emit, "deps": deps, "kind": "d",
                              "di": di})
        self.ndma[eng] += 1
        return ("d", eng, di)

    def coll(self, emit, deps=()):
        ci = self.ncc
        self.ncc += 1
        self.seq["gpsimd"].append({"emit": emit, "deps": list(deps),
                                   "kind": "cc", "ci": ci})
        return ("cc", ci)

    def finalize_and_emit(self, nc, stack):
        for e in ENGS:
            for o in self.seq[e]:
                for dep in o["deps"]:
                    if dep[0] == "e":
                        self.seq[dep[1]][dep[2]]["inc"] = True
        inc_at = {}
        for e in ENGS:
            n = 0
            for i, o in enumerate(self.seq[e]):
                if o["kind"] == "c" and o["inc"]:
                    n += 1
                inc_at[(e, i)] = n
        esem = {e: stack.enter_context(nc.semaphore(f"se_{e}")) for e in ENGS}
        dsem = {e: [stack.enter_context(nc.semaphore(f"sd_{e}_{k}"))
                    for k in range(NDMASEM)]
                for e in ENGS if self.ndma[e] > 0}
        csem = [stack.enter_context(nc.semaphore(f"scc_{j}"))
                for j in range(self.ncc)]

        def resolve(dep):
            if dep[0] == "e":
                _, e, i = dep
                assert self.seq[e][i]["inc"]
                return (("e", e), esem[e], inc_at[(e, i)])
            if dep[0] == "d":
                _, e, di = dep
                return (("d", e, di % NDMASEM), dsem[e][di % NDMASEM],
                        16 * (di // NDMASEM + 1))
            _, ci = dep
            return (("cc", ci), csem[ci], 1)

        block = stack.enter_context(nc.Block())

        def run_engine(engobj, ename):
            waited = {}
            for o in self.seq[ename]:
                for dep in o["deps"]:
                    key, sem, val = resolve(dep)
                    if waited.get(key, 0) < val:
                        engobj.wait_ge(sem, val)
                        waited[key] = val
                inst = o["emit"](engobj)
                if o["kind"] == "d":
                    inst.then_inc(dsem[ename][o["di"] % NDMASEM], 16)
                elif o["kind"] == "cc":
                    inst.then_inc(csem[o["ci"]], 1)
                elif o["inc"]:
                    inst.then_inc(esem[ename], 1)

        block.sync(lambda e: run_engine(e, "sync"))
        block.gpsimd(lambda e: run_engine(e, "gpsimd"))
        block.vector(lambda e: run_engine(e, "vector"))
        block.scalar(lambda e: run_engine(e, "scalar"))
        block.tensor(lambda e: run_engine(e, "tensor"))


def build_nc():
    nc = bass.Bass("TRN2", target_bir_lowering=False, debug=False)
    g = _G()

    x_full = nc.dram_tensor("x_full", [TB, D], F32, kind="ExternalInput").ap()
    x_mine = nc.dram_tensor("x_mine", [TM, D], F32, kind="ExternalInput").ap()
    wqt = nc.dram_tensor("wqt", [D, 1024], F32, kind="ExternalInput").ap()
    wkt = nc.dram_tensor("wkt", [D, 1024], F32, kind="ExternalInput").ap()
    wvt = nc.dram_tensor("wvt", [D, 1024], F32, kind="ExternalInput").ap()
    wot = nc.dram_tensor("wot", [1024, D], F32, kind="ExternalInput").ap()
    lq1 = nc.dram_tensor("lq1", [1, 64], F32, kind="ExternalInput").ap()
    lk1 = nc.dram_tensor("lk1", [1, 64], F32, kind="ExternalInput").ap()
    lq2 = nc.dram_tensor("lq2", [1, 64], F32, kind="ExternalInput").ap()
    lk2 = nc.dram_tensor("lk2", [1, 64], F32, kind="ExternalInput").ap()
    subln = nc.dram_tensor("subln", [1, 128], F32, kind="ExternalInput").ap()
    fbias = nc.dram_tensor("fbias", [1, D], F32, kind="ExternalInput").ap()
    fct = nc.dram_tensor("fct", [D, D], F32, kind="ExternalInput").ap()
    out_e = nc.dram_tensor("out", [TM, D], F32, kind="ExternalOutput").ap()

    xnm_dram = nc.dram_tensor("xnm_dram", [TM, D], BF16).ap()
    q_dram = nc.dram_tensor("q_dram", [NH, 128, TB], BF16).ap()
    k_dram = nc.dram_tensor("k_dram", [NH, 128, TB], BF16).ap()
    attn_dram = nc.dram_tensor("attn_dram", [NH, TB, 128], BF16).ap()
    fct_b = nc.dram_tensor("fct_b", [D, D], F32).ap()
    f_sum = nc.dram_tensor("f_sum", [D, D], F32, addr_space="Shared").ap()
    f_bf = nc.dram_tensor("f_bf", [D, D], BF16).ap()
    rs_in = [nc.dram_tensor(f"rs_in{h}", [TB, 1024], BF16).ap() for h in (0, 1)]
    outA = nc.dram_tensor("outA", [TM, D], F32).ap()
    rs_out = [nc.dram_tensor(f"rs_out{h}", [TM, 1024], BF16).ap() for h in (0, 1)]

    SB = 16512
    TOP = 229344

    def sb(name, shape, dtype, off):
        nbytes = shape[1] * mybir.dt.size(dtype)
        assert off + nbytes <= TOP, (name, off, nbytes)
        return nc.alloc_sbuf_tensor_at(name, shape, dtype, offset=off).ap()

    class Bump:
        def __init__(self, base):
            self.off = base

        def __call__(self, name, shape, dtype):
            nbytes = shape[1] * mybir.dt.size(dtype)
            off = (self.off + 31) // 32 * 32
            self.off = off + nbytes
            return sb(name, shape, dtype, off)

    pz = Bump(SB)
    ones_f = pz("ones_f", [1, 128], F32)
    sub_row = pz("sub_row", [1, 128], F32)
    lamv = pz("lamv", [1, 16], F32)
    lpr1 = pz("lpr1", [1, 64], F32)
    lpr2 = pz("lpr2", [1, 64], F32)
    lnb = pz("lnb", [128, 1], F32)
    swb = pz("swb", [128, 128], F32)
    _fb_off = (pz.off + 31) // 32 * 32
    fb_row = pz("fb_row", [1, D], F32)
    fb_b = pz("fb_b", [128, D], F32)
    snm = pz("snm", [128, 24], F32)
    sss = pz("sss", [128, 24], F32)
    sm2 = pz("sm2", [128, 24], F32)
    svar = pz("svar", [128, 24], F32)
    srst = pz("srst", [128, 24], F32)
    er1 = pz("er1", [128, 4], F32)
    er2 = pz("er2", [128, 4], F32)
    elr = pz("elr", [128, 4], F32)
    ess = pz("ess", [128, 4], F32)
    elnv = pz("elnv", [128, 4], F32)
    ersr = pz("ersr", [128, 4], F32)
    eu1 = [pz(f"eu1_{p}", [128, 128], F32) for p in (0, 1)]
    eac = [pz(f"eac_{p}", [128, 128], F32) for p in (0, 1)]
    esq = sb("esq", [128, 128], F32, _fb_off)
    epsb = pz("epsb", [128, 1], F32)
    idt = pz("idt", [128, 128], F32)

    A_ = (pz.off + 31) // 32 * 32
    B_ = A_ + 65536
    C_ = B_ + 32768
    D_ = C_ + 33280
    S_ = D_ + 32768
    assert S_ + 26624 <= TOP, (S_, TOP)

    xnT_all = sb("xnT_all", [128, 16 * TB], BF16, A_)
    xnT = [xnT_all[:, dc * TB:(dc + 1) * TB] for dc in range(16)]
    ptil = [sb(f"ptil_{p}", [128, 16 * 1024], BF16, A_ + p * 32768) for p in (0, 1)]
    F_sb = [sb(f"F_{dc}", [128, D], BF16, A_ + dc * 4096) for dc in range(16)]
    wtb = [sb(f"wtb_{s}", [128, 16 * 512], BF16, B_ + s * 16384) for s in (0, 1)]
    woT = [sb(f"woT_{h}", [128, D], BF16, B_ + h * 4096) for h in range(NH)]

    v_sb = [sb(f"v_{sc}", [128, 1032], BF16, C_ + sc * 2080) for sc in range(16)]
    acT = [sb(f"acT_{h}", [128, TB], BF16, C_ + h * 4096) for h in range(NH)]
    ob_bf = [sb(f"ob_{p}", [128, D], F32, C_ + p * 8192) for p in (0, 1)]
    rso_t = [[sb(f"rso{ab}_{p}", [128, 1024], BF16,
                 C_ + 16384 + (ab * 2 + p) * 2048) for p in (0, 1)]
             for ab in (0, 1)]
    xb_t = [sb(f"xb_{p}", [128, D], F32, D_ + p * 8192) for p in (0, 1)]
    xs_s2 = [sb("xs_s", [128, D], F32, D_ + 16384),
             sb("xs_sb", [128, D], F32, C_)]
    sq_scr = sb("sq_scr", [128, D], BF16, D_ + 24576)
    fstg = [sb(f"fstg_{p}", [128, D], F32, D_ + 16384 + p * 8192) for p in (0, 1)]
    rs_stage = [sb(f"rsst_{p}", [128, 1024], BF16, D_ + p * 2048) for p in (0, 1)]
    eu1 += [sb(f"eu1_{p}", [128, 128], F32, D_ + (p - 2) * 1024) for p in (2, 3)]
    eac = eac + [sb(f"eac_{p}", [128, 128], F32, D_ + (p - 2) * 1024 + 512)
                 for p in (2, 3)]
    osb = [sb(f"osb_{p}", [128, D], F32, D_ + 8192 + p * 8192) for p in (0, 1)]
    qh_t = [sb(f"qh_{p}", [128, TB], BF16, S_ + p * 4096) for p in (0, 1)]
    xmT = ([sb(f"xmT_{k}", [128, TM], BF16, S_ + k * 2048) for k in range(13)]
           + [sb(f"xmT_{k}", [128, TM], BF16, D_ + 4096 + (k - 13) * 2048)
              for k in (13, 14)]
           + [sb("xmT_15", [128, TM], BF16, _fb_off + 512)])
    xs2 = [sb("xs2_0", [128, D], BF16, S_ + 0),
           sb("xs2_1", [128, D], BF16, S_ + 16384)]
    kh_t = [sb(f"kh_{p}", [128, TB], BF16, S_ + 8192 + p * 4096) for p in (0, 1)]
    ab_t = sb("ab_t", [128, 16 * 128], BF16, S_ + 16384)
    st_t = [sb(f"st_{p}", [128, 512], BF16, S_ + 20480 + p * 1024) for p in (0, 1)]
    wst = [sb(f"wst_{p}", [128, 512], F32, S_ + 22528 + p * 2048) for p in (0, 1)]
    sub_sc = sub_row

    pool_h = nc.alloc_psum_tensor("pool", [128, 4096], F32)
    pool = pool_h.ap()

    def bank(i, lo=0, hi=512):
        return pool[:, i * 512 + lo: i * 512 + hi]

    ident_d = nc.inline_tensor(np.eye(128, dtype=np.float32), "ident128").ap()

    xld = [None] * 16
    for i0 in (0, 1):
        xld[i0] = g.dma("scalar", (lambda i0: lambda e: e.dma_start(
            out=xb_t[i0][:, :], in_=x_full[i0 * 128:(i0 + 1) * 128, :]))(i0))
    t_idt = g.dma("sync", lambda e: e.dma_start(out=idt[:, :], in_=ident_d[:, :]))
    t_subrow = g.dma("sync", lambda e: e.dma_start(out=sub_row[:, :], in_=subln[:, :]))
    t_lq1 = g.dma("sync", lambda e: e.dma_start(out=lpr1[:, :], in_=lq1[:, :]))
    t_lk1 = g.dma("sync", lambda e: e.dma_start(out=lpr2[:, :], in_=lk1[:, :]))
    t_fbrow = g.dma("sync", lambda e: e.dma_start(out=fb_row[:, :], in_=fbias[:, :]))

    t_ones = g.op("vector", lambda e: e.memset(ones_f[:, :], 1.0))
    t_eps = g.op("vector", lambda e: e.memset(epsb[:, :], EPS))
    t_m1 = g.op("vector", lambda e: e.tensor_mul(lpr1[:, :], lpr1[:, :], lpr2[:, :]),
                deps=[t_lq1, t_lk1])
    t_s1 = g.op("vector", lambda e: e.reduce_sum(lamv[:, 0:1], lpr1[:, :], axis=AX.X),
                deps=[t_m1])
    t_lq2 = g.dma("sync", lambda e: e.dma_start(out=lpr1[:, :], in_=lq2[:, :]),
                  deps=[t_s1])
    t_lk2 = g.dma("sync", lambda e: e.dma_start(out=lpr2[:, :], in_=lk2[:, :]),
                  deps=[t_s1])
    t_m2 = g.op("vector", lambda e: e.tensor_mul(lpr1[:, :], lpr1[:, :], lpr2[:, :]),
                deps=[t_lq2, t_lk2])
    t_s2 = g.op("vector", lambda e: e.reduce_sum(lamv[:, 1:2], lpr1[:, :], axis=AX.X),
                deps=[t_m2])
    t_e1 = g.op("scalar", lambda e: e.activation(lamv[:, 2:3], lamv[:, 0:1], AF.Exp),
                deps=[t_s1])
    t_e2 = g.op("scalar", lambda e: e.activation(lamv[:, 3:4], lamv[:, 1:2], AF.Exp),
                deps=[t_s2])
    t_ln0 = g.op("vector", lambda e: e.tensor_sub(lamv[:, 4:5], lamv[:, 3:4], lamv[:, 2:3]),
                 deps=[t_e1, t_e2])
    t_ln1 = g.op("vector", lambda e: e.tensor_scalar(
        out=lamv[:, 5:6], in0=lamv[:, 4:5], scalar1=-LAM_INIT, scalar2=None,
        op0=ALU.add), deps=[t_ln0])
    t_subsc = g.op("vector", lambda e: e.tensor_scalar(
        out=sub_sc[:, :], in0=sub_row[:, :], scalar1=(1.0 - LAM_INIT), scalar2=None,
        op0=ALU.mult), deps=[t_subrow])
    t_bsw = g.op("tensor", lambda e: e.matmul(
        bank(7, 0, 128), ones_f[0:1, :], sub_sc[0:1, :], start=True, stop=True),
        deps=[t_ones, t_subsc])
    t_bln = g.op("tensor", lambda e: e.matmul(
        bank(7, 128, 129), ones_f[0:1, :], lamv[0:1, 5:6], start=True, stop=True),
        deps=[t_ones, t_ln1])
    t_esw = g.op("vector", lambda e: e.tensor_copy(swb[:, :], bank(7, 0, 128)),
                 deps=[t_bsw, t_bln])
    t_eln = g.op("vector", lambda e: e.tensor_copy(lnb[:, :], bank(7, 128, 129)),
                 deps=[t_bln])
    t_fbb = []
    prev = None
    for eo in range(4):
        dep = [t_ones, t_fbrow] + ([prev] if prev else [])
        tb_ = g.op("tensor", (lambda eo: lambda e: e.matmul(
            bank(6), ones_f[0:1, :], fb_row[0:1, eo * 512:(eo + 1) * 512],
            start=True, stop=True))(eo), deps=dep)
        te_ = g.op("vector", (lambda eo: lambda e: e.tensor_copy(
            fb_b[:, eo * 512:(eo + 1) * 512], bank(6)))(eo), deps=[tb_])
        t_fbb.append(te_)
        prev = te_

    norm_tok = [None] * 24
    sq_tok = [None] * 24
    store_tok = [None] * 24
    prev_sq = [None]
    prev_psq = [None]
    xs_war = [None, None]
    xm_war = [None, None]
    xnT_ev = {}
    banklast = [None] * 8
    tb_rot = [0]

    def ln_tile(i):
        mine = i >= 16
        j = i - 16 if mine else i
        src_ = x_mine if mine else x_full
        xb = xb_t[i % 2]
        c = i
        if mine:
            war = [norm_tok[i - 2], sq_tok[i - 2]]
            t_ld = g.dma("sync", (lambda xb, src_, j: lambda e: e.dma_start(
                out=xb[:, :], in_=src_[j * 128:(j + 1) * 128, :]))(xb, src_, j),
                deps=war)
        else:
            t_ld = xld[i]
        ceng = "vector"
        t_sum = g.op(ceng, (lambda xb, c: lambda e: e.reduce_sum(
            snm[:, c:c + 1], xb[:, :], axis=AX.X))(xb, c), deps=[t_ld])
        t_nm = g.op(ceng, (lambda c: lambda e: e.tensor_scalar(
            out=snm[:, c:c + 1], in0=snm[:, c:c + 1], scalar1=-1.0 / D,
            scalar2=None, op0=ALU.mult))(c), deps=[t_sum])
        sqdeps = [t_ld] + ([prev_sq[0]] if prev_sq[0] else [])
        t_sq = g.op("scalar", (lambda xb, c: lambda e: e.activation(
            sq_scr[:, :], xb[:, :], AF.Square, accum_out=sss[:, c:c + 1]))(xb, c),
            deps=sqdeps)
        prev_sq[0] = t_sq
        sq_tok[i] = t_sq
        t_mm2 = g.op(ceng, (lambda c: lambda e: e.tensor_scalar(
            out=sm2[:, c:c + 1], in0=snm[:, c:c + 1], scalar1=snm[:, c:c + 1],
            scalar2=None, op0=ALU.mult))(c), deps=[t_nm])
        t_var = g.op(ceng, (lambda c: lambda e: e.scalar_tensor_tensor(
            out=svar[:, c:c + 1], in0=sss[:, c:c + 1], scalar=1.0 / D,
            in1=sm2[:, c:c + 1], op0=ALU.mult, op1=ALU.subtract))(c),
            deps=[t_sq, t_mm2])
        t_lnv = g.op("scalar", (lambda c: lambda e: e.activation(
            svar[:, c:c + 1], svar[:, c:c + 1], AF.Ln, bias=epsb[:, :]))(c),
            deps=[t_var, t_eps])
        t_rst = g.op("scalar", (lambda c: lambda e: e.activation(
            srst[:, c:c + 1], svar[:, c:c + 1], AF.Exp, scale=-0.5))(c), deps=[t_lnv])
        if mine:
            t_rst = g.op("vector", (lambda c: lambda e: e.tensor_scalar(
                out=srst[:, c:c + 1], in0=srst[:, c:c + 1], scalar1=INV_SQRT8,
                scalar2=None, op0=ALU.mult))(c), deps=[t_rst])
        if mine:
            xso = xs2[j % 2]
            ndeps = [t_nm, t_rst]
            if xm_war[j % 2] is not None:
                ndeps.append(xm_war[j % 2])
            t_no = g.op(ceng, (lambda xb, c, xso: lambda e: e.tensor_scalar(
                out=xso[:, :], in0=xb[:, :], scalar1=snm[:, c:c + 1],
                scalar2=srst[:, c:c + 1], op0=ALU.add, op1=ALU.mult))(xb, c, xso),
                deps=ndeps)
            norm_tok[i] = t_no
            t_st = g.dma("sync", (lambda j, xso: lambda e: e.dma_start(
                out=xnm_dram[j * 128:(j + 1) * 128, :], in_=xso[:, :]))(j, xso),
                deps=[t_no])
            store_tok[i] = t_st
            xm_war[j % 2] = t_st
            return
        xp = i % 2
        xs_s = xs_s2[xp]
        ndeps = [t_nm, t_rst]
        if xs_war[xp] is not None:
            ndeps.append(xs_war[xp])
        t_no = g.op(ceng, (lambda xb, c, xs_s: lambda e: e.tensor_scalar(
            out=xs_s[:, :], in0=xb[:, :], scalar1=snm[:, c:c + 1],
            scalar2=srst[:, c:c + 1], op0=ALU.add, op1=ALU.mult))(xb, c, xs_s),
            deps=ndeps)
        norm_tok[i] = t_no
        if i + 2 < 16:
            xld[i + 2] = g.dma("scalar", (lambda i2: lambda e: e.dma_start(
                out=xb_t[i2 % 2][:, :], in_=x_full[i2 * 128:(i2 + 1) * 128, :]))
                (i + 2), deps=[t_no, t_sq])
        if True:
            last_tr = None
            for dcg in range(4):
                base = ((i * 4 + dcg) % 2) * 4
                trs = []
                for k in range(4):
                    dcb = dcg * 4 + k
                    tb = base + k
                    tdeps = [t_no, t_idt]
                    if banklast[tb] is not None:
                        tdeps.append(banklast[tb])
                    t_tr = g.op("tensor", (lambda dcb, tb, xs_s: lambda e: e.transpose(
                        bank(tb, 0, 128), xs_s[:, dcb * 128:(dcb + 1) * 128],
                        idt[:, :]))(dcb, tb, xs_s), deps=tdeps)
                    trs.append(t_tr)
                    last_tr = t_tr
                t_evt = g.op("scalar", (lambda dcg, i, base: lambda e: e.copy(
                    xnT_all.rearrange("p (dcb c) -> p dcb c", c=TB)
                        [:, dcg * 4:(dcg + 1) * 4, i * 128:(i + 1) * 128],
                    pool.rearrange("p (b c) -> p b c", c=512)
                        [:, base:base + 4, 0:128]))(dcg, i, base),
                    deps=trs)
                for k in range(4):
                    banklast[base + k] = t_evt
                    xnT_ev[(i, dcg * 4 + k)] = t_evt
            xs_war[xp] = last_tr

    wsrc = (wqt, wkt, wvt)
    wt_cast = {}
    wt_last_reader = [None, None]
    wst_last = [None, None]

    def w_slice(si):
        p, eh = si // 2, si % 2
        slot = si % 2
        casts = []
        for dc in range(16):
            ldeps = []
            if wst_last[dc % 2] is not None:
                ldeps.append(wst_last[dc % 2])
            t_l = g.dma("sync", (lambda p, eh, dc: lambda e: e.dma_start(
                out=wst[dc % 2][:, :],
                in_=wsrc[p][dc * 128:(dc + 1) * 128, eh * 512:(eh + 1) * 512]))(p, eh, dc),
                deps=ldeps)
            t_c = g.op("gpsimd", (lambda slot, dc: lambda e: e.tensor_copy(
                wtb[slot][:, dc * 512:(dc + 1) * 512], wst[dc % 2][:, :]))(slot, dc),
                deps=[t_l])
            wst_last[dc % 2] = t_c
            casts.append(t_c)
        wt_cast[si] = casts

    for i in range(4):
        ln_tile(i)
    w_slice(0)
    for i in range(4, 8):
        ln_tile(i)
    w_slice(1)
    for i in range(8, 16):
        ln_tile(i)
    for i in range(16, 24):
        ln_tile(i)

    t_vmem = []
    for sc in range(16):
        deps = [xs_war[1]] if sc < 5 else []
        t_vmem.append(g.op("vector", (lambda sc: lambda e: e.memset(
            v_sb[sc][:, :], 1.0))(sc), deps=deps))

    w_slice(2)
    w_slice(3)
    w_slice(4)
    w_slice(5)

    t_fstage = g.dma("gpsimd", lambda e: e.dma_start(out=fct_b[:, :], in_=fct[:, :]))
    t_ar = g.coll(lambda e: e.collective_compute(
        "AllReduce", ALU.add, replica_groups=[list(range(8))],
        ins=[fct_b[:, :]], outs=[f_sum[:, :]]), deps=[t_fstage])

    q_store = {}
    k_store = {}
    v_evac = []
    slice_last_mm = {}
    if banklast[6] is None:
        banklast[6] = t_fbb[-1]
    if banklast[7] is None:
        banklast[7] = t_eln
    stage_last = [None, None]

    for si in range(6):
        p, eh = si // 2, si % 2
        slot = si % 2
        tws = wt_cast[si]
        if p < 2:
            for ts in range(4):
                for ec in range(4):
                    b = (ts % 2) * 4 + ec
                    head = eh * 4 + ec
                    grp_deps = tws + [xnT_ev[(ts * 4 + k, dc)]
                                      for k in range(4) for dc in range(16)]
                    if banklast[b] is not None:
                        grp_deps = grp_deps + [banklast[b]]
                    last_mm = None
                    for dc in range(16):
                        tok = g.op("tensor", (lambda slot, dc, ec, ts, b: lambda e:
                            e.matmul(bank(b),
                                     wtb[slot][:, dc * 512 + ec * 128: dc * 512 + (ec + 1) * 128],
                                     xnT[dc][:, ts * 512:(ts + 1) * 512],
                                     start=(dc == 0), stop=(dc == 15)))(slot, dc, ec, ts, b),
                            deps=(grp_deps if dc == 0 else []))
                        last_mm = tok
                    wt_last_reader[slot] = last_mm
                    slice_last_mm[si] = last_mm
                    sb_i = (ts * 4 + ec) % 2
                    ev_deps = [last_mm]
                    if stage_last[sb_i] is not None:
                        ev_deps.append(stage_last[sb_i])
                    t_ev = g.op("vector", (lambda sb_i, b: lambda e: e.tensor_copy(
                        st_t[sb_i][:, :], bank(b)))(sb_i, b), deps=ev_deps)
                    banklast[b] = t_ev
                    dstd = q_dram if p == 0 else k_dram
                    t_sto = g.dma("scalar", (lambda sb_i, dstd, head, ts: lambda e:
                        e.dma_start(out=dstd[head, :, ts * 512:(ts + 1) * 512],
                                    in_=st_t[sb_i][:, :]))(sb_i, dstd, head, ts),
                        deps=[t_ev])
                    stage_last[sb_i] = t_sto
                    store = q_store if p == 0 else k_store
                    store.setdefault(head, []).append(t_sto)
        else:
            for sc in range(16):
                b = sc % 8
                grp_deps = tws + [xnT_ev[((sc // 4) * 4 + k, dc)]
                                  for k in range(4) for dc in range(16)]
                if banklast[b] is not None:
                    grp_deps = grp_deps + [banklast[b]]
                last_mm = None
                for dc in range(16):
                    tok = g.op("tensor", (lambda slot, dc, sc, b: lambda e:
                        e.matmul(bank(b),
                                 xnT[dc][:, sc * 128:(sc + 1) * 128],
                                 wtb[slot][:, dc * 512:(dc + 1) * 512],
                                 start=(dc == 0), stop=(dc == 15)))(slot, dc, sc, b),
                        deps=(grp_deps if dc == 0 else []))
                    last_mm = tok
                wt_last_reader[slot] = last_mm
                slice_last_mm[si] = last_mm
                t_ev = g.op("vector", (lambda sc, b, eh: lambda e: e.tensor_copy(
                    v_sb[sc][:, eh * 516: eh * 516 + 516]
                        .rearrange("p (h q) -> p h q", q=129)[:, :, 0:128],
                    bank(b).rearrange("p (h q) -> p h q", q=128)))(sc, b, eh),
                    deps=[last_mm, t_vmem[sc]])
                banklast[b] = t_ev
                v_evac.append(t_ev)

    for si in range(2, 6):
        first = wt_cast[si][0]
        g.seq[first[1]][first[2]]["deps"].append(slice_last_mm[si - 2])

    qk_tok = {}
    qh_last = [None, None]
    kh_last = [None, None]

    def load_head(h, extra=()):
        hp = h % 2
        qdeps = q_store[h] + list(extra) + ([qh_last[hp]] if qh_last[hp] else [])
        t_qh = g.dma("sync", (lambda h, hp: lambda e: e.dma_start(
            out=qh_t[hp][:, :], in_=q_dram[h, :, :]))(h, hp), deps=qdeps)
        kdeps = k_store[h] + ([kh_last[hp]] if kh_last[hp] else [])
        t_kh = g.dma("sync", (lambda h, hp: lambda e: e.dma_start(
            out=kh_t[hp][:, :], in_=k_dram[h, :, :]))(h, hp), deps=kdeps)
        qk_tok[h] = (t_qh, t_kh)

    load_head(0, extra=[store_tok[22]])
    load_head(1)

    a_free = [wt_last_reader[0], wt_last_reader[1]]
    ab_free = [store_tok[23]]
    ab_last = [None]
    epi_last = [None, None, None, None]
    ptil_last = [None, None]
    attn_store = [None] * NH
    sq_free = [t_fbb[-1]]
    units = [(h, tt) for h in range(NH) for tt in range(4)]
    score_info = {}

    def emit_score_group(ui, j):
        h, tt = units[ui]
        S = 4 * (ui % 2)
        if j == 0 and tt == 0 and h + 1 < NH and (h + 1) not in qk_tok:
            load_head(h + 1)
        hp = h % 2
        t_qh, t_kh = qk_tok[h]
        out = []
        for sc in (2 * j, 2 * j + 1):
            b1 = S + 2 * (sc % 2)
            b2 = b1 + 1
            mm_deps1 = [t_qh, t_kh]
            if banklast[b1] is not None:
                mm_deps1 = mm_deps1 + [banklast[b1]]
            t_s1m = g.op("tensor", (lambda hp, sc, tt, b1, ui=ui: lambda e: e.matmul(
                bank(b1, 0, 512),
                kh_t[hp][0:64, sc * 128:(sc + 1) * 128],
                qh_t[hp][0:64, tt * 512:(tt + 1) * 512],
                start=True, stop=True, tile_position=(0, 0)).annotate(
                f"s1m_u{ui}_sc{sc}"))(hp, sc, tt, b1),
                deps=mm_deps1)
            mm_deps2 = []
            if banklast[b2] is not None:
                mm_deps2 = [banklast[b2]]
            t_s2m = g.op("tensor", (lambda hp, sc, tt, b2: lambda e: e.matmul(
                bank(b2, 0, 512),
                kh_t[hp][64:128, sc * 128:(sc + 1) * 128],
                qh_t[hp][64:128, tt * 512:(tt + 1) * 512],
                start=True, stop=True, tile_position=(64, 0)))(hp, sc, tt, b2),
                deps=mm_deps2)
            out.append((t_s1m, t_s2m, b1))
            if tt == 3 and sc == 15:
                qh_last[hp] = t_s2m
                kh_last[hp] = t_s2m
        return out

    def emit_exps(ui, mms):
        par = ui % 2
        exp_toks = []
        for sc in range(16):
            t_s1m, t_s2m, b1 = mms[sc]
            edeps = [t_s1m, t_s2m]
            if ptil_last[par] is not None:
                edeps.append(ptil_last[par])
            if a_free:
                edeps = edeps + a_free
            t_ex = g.op("scalar", (lambda par, sc, b1, ui=ui: lambda e: e.activation(
                out=ptil[par][:, sc * 1024:(sc + 1) * 1024]
                    .rearrange("p (b c) -> p b c", c=512),
                in_=pool[:, b1 * 512:(b1 + 2) * 512]
                    .rearrange("p (b c) -> p b c", c=512),
                func=AF.Exp, scale=0.125).annotate(f"exp_u{ui}_sc{sc}"))(par, sc, b1),
                deps=edeps)
            exp_toks.append(t_ex)
            banklast[b1] = t_ex
            banklast[b1 + 1] = t_ex
            if sc + 2 < 16:
                nxt = mms[sc + 2][0]
                g.seq[nxt[1]][nxt[2]]["deps"].append(t_ex)
        if a_free:
            a_free.clear()
        score_info[ui] = exp_toks

    def emit_av_chunk(ui, tsub, half, av_first):
        h, tt = units[ui]
        par = ui % 2
        S = 4 * (ui % 2)
        a1 = S + (tsub % 2) * 2
        a2 = a1 + 1
        last = None
        for sc in range(half * 8, half * 8 + 8):
            deps1 = []
            deps2 = []
            if sc == 0:
                deps1 = list(score_info[ui])
                for btok in (banklast[a1], banklast[a2]):
                    if btok is not None:
                        deps1.append(btok)
                deps1 += v_evac
                if banklast[a2] is not None:
                    deps2 = [banklast[a2]]
            tok1 = g.op("tensor", (lambda par, sc, tsub, h, a1: lambda e:
                e.matmul(bank(a1, 0, 129),
                         ptil[par][:, sc * 1024 + tsub * 128: sc * 1024 + (tsub + 1) * 128],
                         v_sb[sc][:, h * 129: (h + 1) * 129],
                         start=(sc == 0), stop=(sc == 15)))(par, sc, tsub, h, a1),
                deps=deps1)
            tok2 = g.op("tensor", (lambda par, sc, tsub, h, a2: lambda e:
                e.matmul(bank(a2, 0, 129),
                         ptil[par][:, sc * 1024 + 512 + tsub * 128: sc * 1024 + 512 + (tsub + 1) * 128],
                         v_sb[sc][:, h * 129: (h + 1) * 129],
                         start=(sc == 0), stop=(sc == 15)))(par, sc, tsub, h, a2),
                deps=deps2)
            last = (tok1, tok2)
        return last

    def emit_dve_chain(ui, tsub, last_mms):
        par = ui % 2
        S = 4 * (ui % 2)
        a1 = S + (tsub % 2) * 2
        a2 = a1 + 1
        last_a1, last_a2 = last_mms
        ptil_last[par] = last_a2
        edeps = [last_a1, last_a2, t_eln]
        if epi_last[tsub] is not None:
            edeps.append(epi_last[tsub])
        t_r1 = g.op("vector", (lambda tsub, a1: lambda e: e.reciprocal(
            er1[:, tsub:tsub + 1], bank(a1, 128, 129)))(tsub, a1), deps=edeps)
        t_r2 = g.op("vector", (lambda tsub, a2: lambda e: e.reciprocal(
            er2[:, tsub:tsub + 1], bank(a2, 128, 129)))(tsub, a2), deps=[t_r1])
        t_lr = g.op("vector", (lambda tsub: lambda e: e.tensor_mul(
            elr[:, tsub:tsub + 1], er2[:, tsub:tsub + 1], lnb[:, :]))(tsub),
            deps=[t_r2])
        t_u1 = g.op("vector", (lambda tsub, a1: lambda e: e.tensor_scalar(
            out=eu1[tsub][:, :], in0=bank(a1, 0, 128),
            scalar1=er1[:, tsub:tsub + 1], scalar2=None, op0=ALU.mult))(tsub, a1),
            deps=[t_lr])
        t_ac = g.op("vector", (lambda tsub, a2: lambda e: e.scalar_tensor_tensor(
            out=eac[tsub][:, :], in0=bank(a2, 0, 128),
            scalar=elr[:, tsub:tsub + 1], in1=eu1[tsub][:, :],
            op0=ALU.mult, op1=ALU.add))(tsub, a2), deps=[t_u1])
        banklast[a1] = t_ac
        banklast[a2] = t_ac
        ssdeps = [t_ac]
        if sq_free[0] is not None:
            ssdeps.append(sq_free[0])
            sq_free[0] = None
        t_ss = g.op("vector", (lambda tsub: lambda e: e.scalar_tensor_tensor(
            out=esq[:, :], in0=eac[tsub][:, :], scalar=1.0, in1=eac[tsub][:, :],
            op0=ALU.mult, op1=ALU.mult, accum_out=ess[:, tsub:tsub + 1]))(tsub),
            deps=ssdeps)
        return t_ss, t_ac

    def emit_epi_act(ui, ss_toks):
        t_lv = g.op("scalar", lambda e: e.activation(
            elnv[:, :], ess[:, :], AF.Ln, scale=1.0 / 128.0, bias=epsb[:, :]),
            deps=ss_toks + [t_eps])
        t_rr = g.op("scalar", lambda e: e.activation(
            ersr[:, :], elnv[:, :], AF.Exp, scale=-0.5), deps=[t_lv])
        return t_rr

    def emit_fins(ui, t_rr, ac_info):
        h, tt = units[ui]
        head_finals = []
        for tsub in range(4):
            abdeps = [t_rr, t_esw, ac_info[tsub]]
            if ab_last[0] is not None and tt == 0 and tsub == 0:
                abdeps.append(ab_last[0])
            if ab_free:
                abdeps += ab_free
                ab_free.clear()
            t_fin = g.op("vector", (lambda tsub, tt: lambda e:
                e.scalar_tensor_tensor(
                    out=ab_t[:, (tt * 4 + tsub) * 128:(tt * 4 + tsub + 1) * 128],
                    in0=eac[tsub][:, :], scalar=ersr[:, tsub:tsub + 1], in1=swb[:, :],
                    op0=ALU.mult, op1=ALU.mult))(tsub, tt), deps=abdeps)
            epi_last[tsub] = t_fin
            head_finals.append(t_fin)
        if tt == 3:
            t_asto = g.dma("scalar", (lambda h: lambda e: e.dma_start(
                out=attn_dram[h].rearrange("(c p) e -> p c e", p=128),
                in_=ab_t[:, :].rearrange("p (c e) -> p c e", e=128)))(h),
                deps=head_finals)
            attn_store[h] = t_asto
            ab_last[0] = t_asto

    mms0 = []
    for j in range(8):
        mms0 += emit_score_group(0, j)
    emit_exps(0, mms0)
    for ui in range(len(units)):
        nxt = ui + 1 < len(units)
        mms = []
        ss_toks = []
        ac_info = []
        for j in range(8):
            if nxt:
                mms += emit_score_group(ui + 1, j)
            tsub, half = j // 2, j % 2
            last_mms = emit_av_chunk(ui, tsub, half, av_first=(j == 0))
            if half == 1:
                t_ss, t_ac = emit_dve_chain(ui, tsub, last_mms)
                ss_toks.append(t_ss)
                ac_info.append(t_ac)
        score_info.pop(ui)
        if nxt:
            emit_exps(ui + 1, mms)
        t_rr = emit_epi_act(ui, ss_toks)
        emit_fins(ui, t_rr, ac_info)

    c_free = [t for t in ptil_last if t is not None]
    fstg_last = [None, None]
    fstg_frees = [[xs_war[0]], [sq_tok[23]]]
    t_woc = []
    for h in range(NH):
        fp = h % 2
        ldeps = list(fstg_frees[fp])
        fstg_frees[fp] = []
        if fstg_last[fp] is not None:
            ldeps.append(fstg_last[fp])
        t_l = g.dma("sync", (lambda h, fp: lambda e: e.dma_start(
            out=fstg[fp][:, :], in_=wot[h * 128:(h + 1) * 128, :]))(h, fp),
            deps=ldeps)
        t_c = g.op("gpsimd", (lambda h, fp: lambda e: e.tensor_copy(
            woT[h][:, :], fstg[fp][:, :]))(h, fp),
            deps=[t_l, wt_last_reader[h % 2]])
        fstg_last[fp] = t_c
        t_woc.append(t_c)

    t_acT = []
    for h in range(NH):
        deps = [attn_store[h]] + c_free
        t_acT.append(g.dma("sync", (lambda h: lambda e: e.dma_start(
            out=acT[h][:, :], in_=attn_dram[h], transpose=True))(h), deps=deps))

    t_fc = g.dma("gpsimd", lambda e: e.dma_start(out=f_bf[:, :], in_=f_sum[:, :]),
                 deps=[t_ar])
    af_deps = [t for t in ptil_last if t is not None]
    t_F = []
    for dc in range(16):
        t_F.append(g.dma("sync", (lambda dc: lambda e: e.dma_start(
            out=F_sb[dc][:, :], in_=f_bf[dc * 128:(dc + 1) * 128, :]))(dc),
            deps=[t_fc] + af_deps))

    rs_tok = [None, None]
    rstage_last = [None, None]
    rs_store_tok = [[], []]

    xmT_frees = {0: [qh_last[0]], 1: [qh_last[0]], 2: [qh_last[1]],
                 3: [qh_last[1]], 4: [kh_last[0]], 5: [kh_last[0]],
                 6: [kh_last[1]], 7: [kh_last[1]], 8: [ab_last[0]],
                 9: [ab_last[0]], 10: [stage_last[0], stage_last[1]],
                 11: [wst_last[0]], 12: [wst_last[1]],
                 13: [norm_tok[22], sq_tok[22]], 14: [norm_tok[22], sq_tok[22]],
                 15: [t_fbb[-1]]}
    t_xmT = []
    for dc in range(16):
        t_xmT.append(g.dma("sync", (lambda dc: lambda e: e.dma_start(
            out=xmT[dc][:, :], in_=xnm_dram[:, dc * 128:(dc + 1) * 128],
            transpose=True))(dc),
            deps=[store_tok[16 + k] for k in range(8)] + xmT_frees[dc]))

    def p4_group(eoh, tc):
        tp = tc % 2
        grp = []
        for eo2 in range(2):
            b = tp * 2 + eo2
            deps0 = t_acT + t_woc
            if banklast[b] is not None:
                deps0 = deps0 + [banklast[b]]
            last_mm = None
            for ea in range(NH):
                tok = g.op("tensor", (lambda ea, tc, eoh, eo2, b: lambda e:
                    e.matmul(bank(b),
                             acT[ea][:, tc * 128:(tc + 1) * 128],
                             woT[ea][:, (eoh * 1024 + eo2 * 512):(eoh * 1024 + (eo2 + 1) * 512)],
                             start=(ea == 0), stop=(ea == 7)))(ea, tc, eoh, eo2, b),
                    deps=(deps0 if ea == 0 else []))
                last_mm = tok
            grp.append((last_mm, b))
        evdeps = [t for t, _ in grp]
        if rstage_last[tp] is not None:
            evdeps.append(rstage_last[tp])
        else:
            evdeps += [norm_tok[22], sq_tok[22], epi_last[2], epi_last[3]]
        t_ev0 = g.op("vector", (lambda tp, b0=grp[0][1]: lambda e:
            e.tensor_copy(rs_stage[tp][:, 0:512], bank(b0)))(tp), deps=evdeps)
        t_ev1 = g.op("vector", (lambda tp, b1=grp[1][1]: lambda e:
            e.tensor_copy(rs_stage[tp][:, 512:1024], bank(b1)))(tp), deps=[t_ev0])
        banklast[grp[0][1]] = t_ev1
        banklast[grp[1][1]] = t_ev1
        t_rst_ = g.dma("scalar", (lambda tp, eoh, tc: lambda e: e.dma_start(
            out=rs_in[eoh][tc * 128:(tc + 1) * 128, :], in_=rs_stage[tp][:, :]))(tp, eoh, tc),
            deps=[t_ev1])
        rstage_last[tp] = t_rst_
        rs_store_tok[eoh].append(t_rst_)
        return grp

    for tc in range(16):
        grp_tok = p4_group(0, tc)
    rs_tok[0] = g.coll(lambda e: e.collective_compute(
        "ReduceScatter", ALU.add,
        replica_groups=[[0, 1], [2, 3], [4, 5], [6, 7]],
        ins=[rs_in[0][:, :]], outs=[rs_out[0][:, :]]), deps=rs_store_tok[0])

    xm_tok = [None, None]
    oA_store = [None]
    osb0_free = [norm_tok[23], sq_tok[23]]
    fstg_read = [None, None]

    def ffn_combineA(tcf):
        tp = tcf % 2
        xdeps = [fstg_last[tp]]
        if xm_tok[tp] is not None:
            xdeps.append(xm_tok[tp])
        t_xm = g.dma("sync", (lambda tcf, tp: lambda e: e.dma_start(
            out=fstg[tp][:, :], in_=x_mine[tcf * 128:(tcf + 1) * 128, :]))(tcf, tp),
            deps=xdeps)
        mm_last = []
        for eo in range(4):
            b = 4 + eo
            deps0 = t_F + t_xmT
            if banklast[b] is not None:
                deps0 = deps0 + [banklast[b]]
            last_mm = None
            for dc in range(16):
                tok = g.op("tensor", (lambda dc, tcf, eo, b: lambda e: e.matmul(
                    bank(b), xmT[dc][:, tcf * 128:(tcf + 1) * 128],
                    F_sb[dc][:, eo * 512:(eo + 1) * 512],
                    start=(dc == 0), stop=(dc == 15)))(dc, tcf, eo, b),
                    deps=(deps0 if dc == 0 else []))
                last_mm = tok
            mm_last.append((last_mm, b))
        o2s = []
        for eo in range(4):
            last_mm, b = mm_last[eo]
            deps = [last_mm, t_fbb[eo]]
            if oA_store[0] is not None:
                deps.append(oA_store[0])
            else:
                deps += osb0_free
            t_o1 = g.op("vector", (lambda eo, b: lambda e: e.scalar_tensor_tensor(
                out=osb[0][:, eo * 512:(eo + 1) * 512], in0=bank(b), scalar=1.0,
                in1=fb_b[:, eo * 512:(eo + 1) * 512], op0=ALU.mult, op1=ALU.add))(eo, b),
                deps=deps)
            banklast[b] = t_o1
            t_o2 = g.op("vector", (lambda tp, eo: lambda e: e.tensor_add(
                osb[0][:, eo * 512:(eo + 1) * 512],
                osb[0][:, eo * 512:(eo + 1) * 512],
                fstg[tp][:, eo * 512:(eo + 1) * 512]))(tp, eo), deps=[t_o1, t_xm])
            o2s.append(t_o2)
        fstg_read[tp] = o2s[-1]
        xm_tok[tp] = o2s[-1]
        t_oA = g.dma("scalar", (lambda tcf: lambda e: e.dma_start(
            out=outA[tcf * 128:(tcf + 1) * 128, :], in_=osb[0][:, :]))(tcf),
            deps=o2s)
        oA_store[0] = t_oA

    for tc in range(16):
        p4_group(1, tc)
        if tc % 2 == 1:
            ffn_combineA(tc // 2)
    rs_tok[1] = g.coll(lambda e: e.collective_compute(
        "ReduceScatter", ALU.add,
        replica_groups=[[0, 1], [2, 3], [4, 5], [6, 7]],
        ins=[rs_in[1][:, :]], outs=[rs_out[1][:, :]]), deps=rs_store_tok[1])
    b_free = [banklast[0], banklast[1]]

    out_store = []
    ob_last = [[None, None], [None, None]]
    rso_last = [[None, None], [None, None]]
    osb_last = [[oA_store[0], None], [oA_store[0], None]]
    osb1_free = [[fstg_read[0]], [fstg_read[0]]]
    for ab in range(2):
        lo = ab * 1024
        for tc in range(8):
            tp = tc % 2
            obdeps = [banklast[0], banklast[1], oA_store[0]]
            if ob_last[ab][tp] is not None:
                obdeps.append(ob_last[ab][tp])
            t_ob = g.dma("sync", (lambda tc, tp, lo: lambda e: e.dma_start(
                out=ob_bf[tp][:, lo:lo + 1024],
                in_=outA[tc * 128:(tc + 1) * 128, lo:lo + 1024]))(tc, tp, lo),
                deps=obdeps)
            deps = [rs_tok[ab], banklast[0], banklast[1]]
            if rso_last[ab][tp] is not None:
                deps.append(rso_last[ab][tp])
            t_rs = g.dma("sync", (lambda ab, tc, tp: lambda e: e.dma_start(
                out=rso_t[ab][tp][:, :],
                in_=rs_out[ab][tc * 128:(tc + 1) * 128, :]))(ab, tc, tp), deps=deps)
            st_toks = []
            for eo in (ab * 2, ab * 2 + 1):
                deps = [t_ob, t_rs]
                if osb_last[ab][tp] is not None:
                    deps.append(osb_last[ab][tp])
                if tp == 1 and osb1_free[ab]:
                    deps += osb1_free[ab]
                    osb1_free[ab].clear()
                t_b = g.op("vector", (lambda tp, eo, ab: lambda e: e.tensor_add(
                    osb[tp][:, eo * 512:(eo + 1) * 512],
                    ob_bf[tp][:, eo * 512:(eo + 1) * 512],
                    rso_t[ab][tp][:, (eo % 2) * 512:(eo % 2 + 1) * 512]))(tp, eo, ab),
                    deps=deps)
                st_toks.append(t_b)
            ob_last[ab][tp] = st_toks[-1]
            rso_last[ab][tp] = st_toks[-1]
            t_out = g.dma("scalar", (lambda tp, tc, lo: lambda e: e.dma_start(
                out=out_e[tc * 128:(tc + 1) * 128, lo:lo + 1024],
                in_=osb[tp][:, lo:lo + 1024]))(tp, tc, lo),
                deps=st_toks)
            osb_last[ab][tp] = t_out
            out_store.append(t_out)

    g.op("scalar", lambda e: e.nop(), deps=out_store)

    stack = ExitStack()
    g.finalize_and_emit(nc, stack)
    stack.close()
    return nc


_NC_CACHE = None


def _get_nc():
    global _NC_CACHE
    if _NC_CACHE is None:
        _NC_CACHE = build_nc()
    return _NC_CACHE


def make_in_maps(inputs):
    x = np.ascontiguousarray(np.asarray(inputs["L0_attn_x"], dtype=np.float32))
    Wq = np.asarray(inputs["L0_attn_Wq"], dtype=np.float32)
    Wk = np.asarray(inputs["L0_attn_Wk"], dtype=np.float32)
    Wv = np.asarray(inputs["L0_attn_Wv"], dtype=np.float32)
    Wo = np.asarray(inputs["L0_attn_Wo"], dtype=np.float32)
    lq1 = np.asarray(inputs["L0_attn_lambda_q1"], dtype=np.float32).reshape(1, 64)
    lk1 = np.asarray(inputs["L0_attn_lambda_k1"], dtype=np.float32).reshape(1, 64)
    lq2 = np.asarray(inputs["L0_attn_lambda_q2"], dtype=np.float32).reshape(1, 64)
    lk2 = np.asarray(inputs["L0_attn_lambda_k2"], dtype=np.float32).reshape(1, 64)
    sw = np.asarray(inputs["L0_attn_subln_weight"], dtype=np.float32).reshape(1, 128)
    fb = np.asarray(inputs["L0_ffn_bias"], dtype=np.float32).reshape(1, D)
    fc = np.asarray(inputs["L0_ffn_factors"], dtype=np.float32)
    in_maps = []
    for c in range(8):
        b, gg = c // 2, c % 2
        sl = slice(gg * 1024, (gg + 1) * 1024)
        in_maps.append({
            "x_full": np.ascontiguousarray(x[b]),
            "x_mine": np.ascontiguousarray(x[b, gg * TM:(gg + 1) * TM]),
            "wqt": np.ascontiguousarray(Wq[sl].T),
            "wkt": np.ascontiguousarray(Wk[sl].T),
            "wvt": np.ascontiguousarray(Wv[sl].T),
            "wot": np.ascontiguousarray(Wo[:, sl].T),
            "lq1": lq1, "lk1": lk1, "lq2": lq2, "lk2": lk2,
            "subln": sw, "fbias": fb,
            "fct": np.ascontiguousarray(fc[c]),
        })
    return in_maps


def kernel(**inputs):
    nc = _get_nc()
    in_maps = make_in_maps(inputs)
    res = run_bass_kernel_spmd(nc, in_maps, core_ids=list(range(8)))
    out = np.empty((4, TB, D), dtype=np.float32)
    for c in range(8):
        b, gg = c // 2, c % 2
        out[b, gg * TM:(gg + 1) * TM] = res.results[c]["out"]
    return out
